# version 03
# baseline (speedup 1.0000x reference)
import numpy as np
import ml_dtypes
import concourse.bass as bass
import concourse.mybir as mybir
from concourse.bass_utils import run_bass_kernel_spmd

F32 = mybir.dt.float32
BF16 = mybir.dt.bfloat16
AF = mybir.ActivationFunctionType
ALU = mybir.AluOpType
AX = mybir.AxisListType

P, D, S = 128, 1024, 2048
DC, T = 8, 256
NT = S // T
NSUB = 8
FC = 32
NEXP, SLOT, ES = 16, 64, 1024
EPS_TOP = 1e-5
EPS_SUB = 1e-6
EPS_FFN = 1e-5

WSUB = 98304
WBLOB_COLS = NSUB * WSUB
EXP_COLS = 6 * 8192 + 64
FBLOB_COLS = 8192 + NEXP * EXP_COLS

C_SUME, C_SUMR, C_SUMKV, C_SUMK, C_KV = 0, 8, 16, 24, 32
C_T0, C_T1 = 40, 41
C_SCL, C_SG, C_SB = 42, 43, 51
C_F0, C_F1, C_F2, C_F3, C_F4 = 0, 1, 2, 3, 4

B_BQ, B_BK, B_BV, B_BO, B_B1, B_B2 = 0, 8, 16, 24, 32, 64
B_L1S, B_L1B, B_L2S, B_L2B = 72, 80, 88, 96

FC_NG, FC_NB, FC_SG, FC_SB, FC_SCALE = 0, 8, 16, 24, 32

IB_ONECOL, IB_ONEROW = 128, 160

HB_ROT0, HB_ROT1 = 0, 256
HB_UB0 = 512
HB_SQ0 = 768
HB_RHI, HB_RLO, HB_BB = 1024, 1280, 1536
HB_GU0, HB_GU1 = 1792, 1856

BW0, BW1, BW2, BW3, BW4, BW5 = 0, 256, 512, 768, 1024, 1280


class Prog:
    ENG = ("tensor", "vector", "scalar", "gpsimd", "sync")

    def __init__(self):
        self.ops = []
        self.cnt = {e: 0 for e in self.ENG}
        for s in ("wdS", "wdA", "wdW1", "wdW2a", "wdW2b", "wdF", "wdE0",
                  "wdE1", "xdI", "xdB", "xdR", "xdFB", "xdBD", "out"):
            self.cnt[s] = 0
        self.pending = {e: None for e in self.ENG}
        self.cursor = 0
        self.lastwait = {e: {} for e in self.ENG}

    AUTO = ("vector", "scalar", "gpsimd")

    def op(self, eng, fn, waits=()):
        waits = [w for w in waits if w]
        if eng in self.AUTO:
            if self.cnt[eng] > 0:
                waits.append((eng, self.cnt[eng]))
            self.cnt[eng] += 1
            self.ops.append(dict(eng=eng, fn=fn, waits=waits, inc=(eng, 1)))
            self.pending[eng] = None
        else:
            self.ops.append(dict(eng=eng, fn=fn, waits=waits, inc=None))
            self.pending[eng] = len(self.ops) - 1

    def dma(self, eng, sem, fn, waits=()):
        self.cnt[sem] += 16
        self.ops.append(dict(eng=eng, fn=fn, waits=[w for w in waits if w],
                             inc=(sem, 16)))
        return (sem, self.cnt[sem])

    def fence(self, eng):
        if eng in self.AUTO:
            return (eng, self.cnt[eng])
        i = self.pending[eng]
        if i is not None:
            assert self.ops[i]["inc"] is None
            self.cnt[eng] += 1
            self.ops[i]["inc"] = (eng, 1)
            self.pending[eng] = None
        return (eng, self.cnt[eng])

    def lint(self):
        streams = {e: [] for e in self.ENG}
        for o in self.ops:
            streams[o["eng"]].append(o)
        pos = {e: 0 for e in self.ENG}
        counts = {k: 0 for k in self.cnt}
        progress = True
        while progress:
            progress = False
            for e in self.ENG:
                while pos[e] < len(streams[e]):
                    o = streams[e][pos[e]]
                    if all(counts[s] >= v for (s, v) in o["waits"]):
                        if o["inc"] is not None:
                            counts[o["inc"][0]] += o["inc"][1]
                        pos[e] += 1
                        progress = True
                    else:
                        break
        stuck = {e: (pos[e], len(streams[e])) for e in self.ENG
                 if pos[e] < len(streams[e])}
        if stuck:
            detail = {}
            for e in stuck:
                o = streams[e][pos[e]]
                detail[e] = [(s, v, counts[s]) for (s, v) in o["waits"]
                             if counts[s] < v]
            raise RuntimeError(f"sem deadlock: stuck={stuck} first-waits={detail}")

    class _TProxy:

        def __init__(self, eng):
            self._e = eng

        def matmul(self, *a, **k):
            k.setdefault("skip_group_check", True)
            return self._e.matmul(*a, **k)

        def __getattr__(self, n):
            return getattr(self._e, n)

    def replay(self, block, sems):
        todo = self.ops[self.cursor:]
        self.cursor = len(self.ops)
        for ename in self.ENG:
            myops = [o for o in todo if o["eng"] == ename]
            if not myops:
                continue
            sec = getattr(block, ename)

            def body(eng, myops=myops, ename=ename):
                if ename == "tensor":
                    eng = Prog._TProxy(eng)
                lw = self.lastwait[ename]
                for o in myops:
                    for (s, v) in o["waits"]:
                        if lw.get(s, -1) < v:
                            eng.wait_ge(sems[s], v)
                            lw[s] = v
                    if o["fn"] is not None:
                        instr = o["fn"](eng)
                        if o["inc"] is not None:
                            assert instr is not None
                            instr.then_inc(sems[o["inc"][0]], o["inc"][1])
                    else:
                        assert o["inc"] is None
            sec(body)


def build_nc(n_sub=NSUB, do_ffn=True, skip_readd=False):
    nc = bass.Bass("TRN2", target_bir_lowering=False)
    ap = {}
    ap["ximg"] = nc.dram_tensor("ximg", [P, DC * S], F32, kind="ExternalInput").ap()
    ap["wblob"] = nc.dram_tensor("wblob", [P, WBLOB_COLS], F32, kind="ExternalInput").ap()
    ap["bblob"] = nc.dram_tensor("bblob", [P, NSUB * 128], F32, kind="ExternalInput").ap()
    ap["identb"] = nc.dram_tensor("identb", [P, 288], F32, kind="ExternalInput").ap()
    if do_ffn:
        ap["fblob"] = nc.dram_tensor("fblob", [P, FBLOB_COLS], F32, kind="ExternalInput").ap()
        ap["fcb"] = nc.dram_tensor("fcb", [P, 64], F32, kind="ExternalInput").ap()
        ap["bdimg"] = nc.dram_tensor("bdimg", [SLOT, NEXP * D], F32, kind="ExternalInput").ap()
    ap["y"] = nc.dram_tensor("y", [P, DC * S], F32, kind="ExternalOutput").ap()

    pr = Prog()

    from contextlib import ExitStack
    with ExitStack() as _stk0:
        STREAM = _stk0.enter_context(nc.sbuf_tensor("stream", [P, DC * S], F32))
        XT = _stk0.enter_context(nc.sbuf_tensor("xt", [P, DC * T], BF16))
        BW = _stk0.enter_context(nc.sbuf_tensor("bw", [P, 1536], F32))
        HB = _stk0.enter_context(nc.sbuf_tensor("hb", [P, 1920], BF16))
        CONSTS = _stk0.enter_context(nc.sbuf_tensor("consts", [P, 64], F32))
        BIAS = _stk0.enter_context(nc.sbuf_tensor("biasb", [P, 128], F32))
        IDENTB = _stk0.enter_context(nc.sbuf_tensor("identb_sb", [P, 288], BF16))
        FCONST = _stk0.enter_context(nc.sbuf_tensor("fconst", [P, 64], F32))
        FB = _stk0.enter_context(nc.sbuf_tensor("fbias", [P, 64], F32))
        nc.const_aps.register(F32, float(EPS_TOP), BIAS[:, 104:105])
        nc.const_aps.register(F32, float(EPS_SUB), BIAS[:, 105:106])
        _sem_names = ["tensor", "vector", "scalar", "gpsimd", "sync",
                      "wdS", "wdA", "wdW1", "wdW2a", "wdW2b", "wdF", "wdE0",
                      "wdE1", "xdI", "xdB", "xdR", "xdFB", "xdBD", "out"]
        _sems = [_stk0.enter_context(nc.semaphore(f"sem_{n}")) for n in _sem_names]
        SEMS = dict(zip(_sem_names, _sems))

        def schunk(dc, lo, n):
            return STREAM[:, dc * S + lo: dc * S + lo + n]

        def xtc(dc):
            return XT[:, dc * T:(dc + 1) * T]

        def bwc(b, n=T):
            return BW[:, b:b + n]

        def hbc(b, n=T):
            return HB[:, b:b + n]

        def ccol(c):
            return CONSTS[:, c:c + 1]

        def bcolv(c):
            return BIAS[:, c:c + 1]

        ONES_COL = IDENTB[:, IB_ONECOL:IB_ONECOL + 1]
        ONES_ROW = IDENTB[0:1, IB_ONEROW:IB_ONEROW + 128]
        IDENT = IDENTB[:, 0:128]

        pr.dma("sync", "xdI", lambda e: e.dma_start(out=BW[:, 0:288], in_=ap["identb"]))
        pr.dma("sync", "xdI", lambda e: e.dma_start(out=BIAS[:], in_=ap["bblob"][:, 0:128]))
        if do_ffn:
            pr.dma("sync", "xdI", lambda e: e.dma_start(out=FCONST[:], in_=ap["fcb"]))
        w_init = pr.dma("sync", "xdI", lambda e: e.dma_start(out=STREAM[:], in_=ap["ximg"]))
        pr.op("vector", lambda e: e.tensor_copy(IDENTB[:], BW[:, 0:288]),
              waits=[w_init])
        w_ident = pr.fence("vector")
        w_bias0 = w_init
        w_fcb = w_init
        w_x = w_init

        def emit_stats(stat_bank, src_fn, wait_entry=()):
            for dc in range(DC):
                src = src_fn(dc)
                wv = list(wait_entry) if dc == 0 else [pr.fence("tensor")]
                pr.op("vector", lambda e, src=src: e.tensor_copy(
                    hbc(HB_UB0), src), waits=wv)
                fv = pr.fence("vector")
                pr.op("scalar", lambda e, src=src: e.activation(
                    hbc(HB_SQ0), src, AF.Square),
                    waits=[fv] + (list(wait_entry) if dc == 0 else []))
                fs = pr.fence("scalar")
                pr.op("tensor", lambda e, b=stat_bank, f=(dc == 0),
                      l=(dc == DC - 1): e.matmul(
                          b[0:1, 0:512], ONES_COL, HB[:, HB_UB0:HB_UB0 + 512],
                          start=f, stop=l), waits=[fv, fs])

        def emit_ln_finish(stat_bank, z_bank, eps, affine, src_fn, dst_fn,
                           double_eps2=None):
            ft = pr.fence("tensor")
            XA = stat_bank[0:1, 0:256]
            XB = stat_bank[0:1, 256:512]
            ZB = z_bank[0:1, 256:512]
            R = BW[0:1, BW4:BW4 + 256]

            pr.op("vector", lambda e: e.tensor_scalar(XA, XA, -1.0 / D, None, ALU.mult),
                  waits=[ft])
            pr.op("vector", lambda e: e.tensor_scalar(XB, XB, 1.0 / D, None, ALU.mult))
            fv = pr.fence("vector")
            pr.op("scalar", lambda e: e.activation(R, XA, AF.Square), waits=[fv])
            fs = pr.fence("scalar")
            pr.op("vector", lambda e: e.tensor_tensor(XB, XB, R, ALU.subtract),
                  waits=[fs])
            fv = pr.fence("vector")
            if double_eps2 is not None:
                pr.op("scalar", lambda e: e.activation(ZB, XB, AF.Identity), waits=[fv])
            pr.op("scalar", lambda e: e.activation(XB, XB, AF.Ln, bias=float(eps),
                                                   scale=1.0), waits=[fv])
            pr.op("scalar", lambda e: e.activation(XB, XB, AF.Exp, scale=-0.5))
            if double_eps2 is not None:
                pr.op("scalar", lambda e: e.activation(R, XB, AF.Square))
                fs = pr.fence("scalar")
                pr.op("vector", lambda e: e.tensor_tensor(ZB, ZB, R, ALU.mult),
                      waits=[fs])
                fv = pr.fence("vector")
                pr.op("scalar", lambda e: e.activation(ZB, ZB, AF.Ln,
                                                       bias=float(double_eps2),
                                                       scale=1.0), waits=[fv])
                pr.op("scalar", lambda e: e.activation(ZB, ZB, AF.Exp, scale=-0.5))
                pr.op("scalar", lambda e: e.activation(R, ZB, AF.Identity))
                fs = pr.fence("scalar")
                pr.op("vector", lambda e: e.tensor_tensor(XB, XB, R, ALU.mult),
                      waits=[fs])
            fv = pr.fence("vector")
            pr.op("scalar", lambda e: e.activation(R, XB, AF.Identity), waits=[fv])
            fs = pr.fence("scalar")
            pr.op("vector", lambda e: e.tensor_copy(HB[0:1, HB_RHI:HB_RHI + 256], R),
                  waits=[fs])
            ZA = z_bank[0:1, 0:256]
            pr.op("vector", lambda e: e.tensor_copy(ZA, HB[0:1, HB_RHI:HB_RHI + 256]))
            pr.op("vector", lambda e: e.tensor_tensor(ZA, R, ZA, ALU.subtract))
            pr.op("vector", lambda e: e.tensor_copy(HB[0:1, HB_RLO:HB_RLO + 256], ZA))
            pr.op("vector", lambda e: e.tensor_tensor(XA, XA, R, ALU.mult))
            pr.op("vector", lambda e: e.tensor_copy(HB[0:1, HB_BB:HB_BB + 256], XA))
            frow = pr.fence("vector")
            pr.op("tensor", lambda e, b=z_bank: e.matmul(
                b[:, 0:256], ONES_ROW, HB[0:1, HB_RHI:HB_RHI + 256],
                start=True, stop=False), waits=[frow])
            pr.op("tensor", lambda e, b=z_bank: e.matmul(
                b[:, 0:256], ONES_ROW, HB[0:1, HB_RLO:HB_RLO + 256],
                start=False, stop=True))
            pr.op("tensor", lambda e, b=z_bank: e.matmul(
                b[:, 256:512], ONES_ROW, HB[0:1, HB_BB:HB_BB + 256],
                start=False, stop=True))
            fb = pr.fence("tensor")
            f_id = {}
            for dc in range(DC):
                src = src_fn(dc)
                dst = dst_fn(dc)
                slot = BW4 if dc % 2 == 0 else BW5
                wt_ap = [fb] if dc == 0 else []
                if affine is not None and dc >= 2:
                    wt_ap.append(f_id[dc - 2])
                pr.op("vector", lambda e, src=src, b=z_bank, sl=slot: e.tensor_tensor(
                    bwc(sl), src, b[:, 0:256], ALU.mult), waits=wt_ap)
                pr.op("vector", lambda e, b=z_bank, sl=slot: e.tensor_tensor(
                    bwc(sl), bwc(sl), b[:, 256:512], ALU.add))
                fv3 = pr.fence("vector")
                if affine is not None:
                    gb, bb2, ct = affine
                    pr.op("scalar", lambda e, dst=dst, gb=gb, bb2=bb2, ct=ct, dc=dc,
                          sl=slot: e.activation(dst, bwc(sl), AF.Identity,
                                                bias=ct[:, bb2 + dc:bb2 + dc + 1],
                                                scale=ct[:, gb + dc:gb + dc + 1]),
                          waits=[fv3])
                    f_id[dc] = pr.fence("scalar")
                else:
                    pr.op("vector", lambda e, dst=dst, sl=slot: e.tensor_copy(
                        dst, bwc(sl)))

        def cast_tile(t, waits=()):
            src = STREAM.ap().rearrange("p (c s) -> p c s", c=DC)[:, :, t * T:(t + 1) * T]
            dst = XT.ap().rearrange("p (c s) -> p c s", c=DC)
            pr.op("vector", lambda e, src=src, dst=dst: e.tensor_copy(dst, src),
                  waits=waits)

        from contextlib import ExitStack
        with ExitStack() as _stk1:
            PB0, PB1, PB2, PB3, PB4, PB5, PB6, PB7 = [
                _stk1.enter_context(nc.psum_tensor(f"pb{j}", [P, 512], F32))
                for j in range(8)]
            WREG0 = _stk1.enter_context(nc.sbuf_tensor("wreg0", [P, 32768], BF16))
            WREG1 = _stk1.enter_context(nc.sbuf_tensor("wreg1", [P, 32768], BF16))
            PBX, PBZ = PB6, PB7

            for t in range(NT):
                emit_stats(PBX, lambda dc, t=t: schunk(dc, t * T, T),
                           wait_entry=[w_x] if t == 0 else ())
                emit_ln_finish(PBX, PBZ, EPS_TOP, None,
                               lambda dc, t=t: schunk(dc, t * T, T),
                               lambda dc, t=t: schunk(dc, t * T, T))

            WREGS = [WREG0, WREG1]

            def wk_t(R, dc, hc):
                return R[:, dc * 1024 + hc * 128: dc * 1024 + (hc + 1) * 128]

            def wv_t(R, dc, hc):
                return R[:, 8192 + dc * 1024 + hc * 128: 8192 + dc * 1024 + (hc + 1) * 128]

            def wq_t(R, dc, hc):
                return R[:, 16384 + dc * 1024 + hc * 128: 16384 + dc * 1024 + (hc + 1) * 128]

            def wo_t(R, hc, oc):
                return R[:, 24576 + hc * 1024 + oc * 128: 24576 + hc * 1024 + (oc + 1) * 128]

            def w1_t(R, dc, fc):
                return R[:, dc * 4096 + fc * 128: dc * 4096 + (fc + 1) * 128]

            def w2_t(R, fc, oc):
                return R[:, fc * 1024 + oc * 128: fc * 1024 + (oc + 1) * 128]

            reg_free = {(0, 0): [], (0, 1): [], (1, 0): [], (1, 1): []}
            bias_free = []

            for i in range(n_sub):
                Ra, Rb = WREGS[i % 2], WREGS[(i + 1) % 2]
                base = i * WSUB
                w_kv = pr.dma("gpsimd", "wdS", lambda e, b=base, R=Ra: e.dma_start(
                    out=R[:, 0:16384], in_=ap["wblob"][:, b:b + 16384]),
                    waits=reg_free[(i % 2, 0)])
                w_qo = pr.dma("gpsimd", "wdA", lambda e, b=base, R=Ra: e.dma_start(
                    out=R[:, 16384:32768], in_=ap["wblob"][:, b + 16384:b + 32768]),
                    waits=reg_free[(i % 2, 1)])
                pr.dma("gpsimd", "wdW1", lambda e, b=base, R=Rb: e.dma_start(
                    out=R[:, 0:16384], in_=ap["wblob"][:, b + 32768:b + 49152]),
                    waits=reg_free[((i + 1) % 2, 0)])
                w_w1 = pr.dma("gpsimd", "wdW1", lambda e, b=base, R=Rb: e.dma_start(
                    out=R[:, 16384:32768], in_=ap["wblob"][:, b + 49152:b + 65536]),
                    waits=reg_free[((i + 1) % 2, 1)])
                if i > 0:
                    w_bias = pr.dma("sync", "xdB", lambda e, i=i: e.dma_start(
                        out=BIAS[:], in_=ap["bblob"][:, i * 128:(i + 1) * 128]),
                        waits=bias_free)
                else:
                    w_bias = w_bias0

                f_er = {}
                f_pr = {}
                for t in range(NT):
                    cast_tile(t, waits=[pr.fence("tensor"), pr.fence("scalar")])
                    fcast = pr.fence("vector")
                    for p in range(4):
                        for h in (2 * p, 2 * p + 1):
                            pk = PB0 if h % 2 == 0 else PB1
                            pv = PB2 if h % 2 == 0 else PB3
                            g2 = t * 8 + h - 2
                            wt = [fcast]
                            if g2 >= 0:
                                wt += [f_er[g2], f_pr[g2]]
                            for dc in range(DC):
                                pr.op("tensor", lambda e, pk=pk, R=Ra, dc=dc, h=h:
                                      e.matmul(pk[:, 0:256], wk_t(R, dc, h), xtc(dc),
                                               start=(dc == 0), stop=(dc == DC - 1)),
                                      waits=(wt + ([w_kv] if (t == 0 and h == 0) else [])) if dc == 0 else [])
                            for dc in range(DC):
                                pr.op("tensor", lambda e, pv=pv, R=Ra, dc=dc, h=h:
                                      e.matmul(pv[:, 0:256], wv_t(R, dc, h), xtc(dc),
                                               start=(dc == 0), stop=(dc == DC - 1)))
                        fmm = pr.fence("tensor")
                        for h in (2 * p, 2 * p + 1):
                            pk = PB0 if h % 2 == 0 else PB1
                            Ba = BW0 if h % 2 == 0 else BW2
                            Bb = BW1 if h % 2 == 0 else BW3
                            wsc = [fmm, w_bias]
                            pr.op("scalar", lambda e, pk=pk, h=h, B=Ba: e.activation(
                                bwc(B), pk[:, 0:256], AF.Exp,
                                bias=bcolv(B_BK + h), scale=1.0), waits=wsc)
                            pr.op("scalar", lambda e, pk=pk, h=h, B=Bb: e.activation(
                                bwc(B), pk[:, 0:256], AF.Relu,
                                bias=bcolv(B_BK + h), scale=1.0,
                                accum_out=ccol(C_T0 if h % 2 == 0 else C_T1)))
                            f_er[t * 8 + h] = pr.fence("scalar")
                        for h in (2 * p, 2 * p + 1):
                            pv = PB2 if h % 2 == 0 else PB3
                            Ba = BW0 if h % 2 == 0 else BW2
                            Bb = BW1 if h % 2 == 0 else BW3
                            ct = C_T0 if h % 2 == 0 else C_T1
                            first = (t == 0)
                            pr.op("vector", lambda e, h=h, B=Ba, first=first:
                                  e.tensor_scalar(bwc(B), bwc(B), 1.0,
                                                  None if first else ccol(C_SUME + h),
                                                  ALU.min, ALU.add,
                                                  accum_out=ccol(C_SUME + h)),
                                  waits=[f_er[t * 8 + h]])
                            pr.op("vector", lambda e, h=h, ct=ct, first=first:
                                  e.tensor_scalar(ccol(C_SUMR + h), ccol(ct),
                                                  0.0 if first else ccol(C_SUMR + h),
                                                  None, ALU.add))
                            pr.op("vector", lambda e, B=Ba, B2=Bb: e.tensor_tensor(
                                bwc(B), bwc(B), bwc(B2), ALU.add))
                            pr.op("vector", lambda e, pv=pv, B=Ba, B2=Bb:
                                  e.tensor_tensor(bwc(B2), bwc(B), pv[:, 0:256],
                                                  ALU.mult))
                            pr.op("vector", lambda e, B2=Bb, ct=ct: e.reduce_sum(
                                ccol(ct), bwc(B2), axis=AX.X))
                            pr.op("vector", lambda e, h=h, ct=ct, first=first:
                                  e.tensor_scalar(ccol(C_SUMKV + h), ccol(ct),
                                                  0.0 if first else ccol(C_SUMKV + h),
                                                  None, ALU.add))
                            f_pr[t * 8 + h] = pr.fence("vector")
                for hc in range(8):
                    pr.op("vector", lambda e, hc=hc: e.tensor_scalar(
                        ccol(C_SUMK + hc), ccol(C_SUME + hc),
                        ccol(C_SUMR + hc), None, ALU.add),
                        waits=[w_bias] if hc == 0 else ())
                    pr.op("vector", lambda e, hc=hc: e.tensor_scalar(
                        ccol(C_T0), ccol(C_SUMK + hc), bcolv(B_BV + hc),
                        None, ALU.mult))
                    pr.op("vector", lambda e, hc=hc: e.tensor_scalar(
                        ccol(C_KV + hc), ccol(C_T0), ccol(C_SUMKV + hc),
                        None, ALU.add))
                f_after_S = pr.fence("tensor")
                w_w2a = pr.dma("gpsimd", "wdW2a", lambda e, b=base, R=Ra: e.dma_start(
                    out=R[:, 0:16384], in_=ap["wblob"][:, b + 65536:b + 81920]),
                    waits=[f_after_S])

                def A_D1(t):
                    emit_stats(PBX, lambda dc, t=t: schunk(dc, t * T, T))

                def A_D3(t):
                    emit_ln_finish(PBX, PBZ, EPS_SUB, (B_L1S, B_L1B, BIAS),
                                   lambda dc, t=t: schunk(dc, t * T, T),
                                   lambda dc, t=t: schunk(dc, t * T, T))

                f_aev = {}
                f_att = {}
                f_wo = {}
                f_evS = pr.fence("scalar")
                for t in range(NT):
                    cast_tile(t, waits=[pr.fence("tensor"), pr.fence("scalar")])
                    fcast = pr.fence("vector")
                    for p in range(4):
                        for h in (2 * p, 2 * p + 1):
                            pq = PB0 if h % 2 == 0 else PB1
                            g2 = t * 8 + h - 2
                            wt = [fcast] + ([f_aev[g2]] if g2 >= 0 else
                                            [pr.fence("scalar")])
                            for dc in range(DC):
                                pr.op("tensor", lambda e, pq=pq, R=Ra, dc=dc, h=h:
                                      e.matmul(pq[:, 0:256], wq_t(R, dc, h),
                                               xtc(dc), start=(dc == 0),
                                               stop=(dc == DC - 1)),
                                      waits=(wt + ([w_qo] if (t == 0 and h == 0) else [])) if dc == 0 else [])
                        fmm = pr.fence("tensor")
                        for h in (2 * p, 2 * p + 1):
                            pq = PB0 if h % 2 == 0 else PB1
                            Ba = BW0 if h % 2 == 0 else BW2
                            Bb = BW1 if h % 2 == 0 else BW3
                            g2 = t * 8 + h - 2
                            wsc = [fmm]
                            if g2 >= 0:
                                wsc.append(f_att[g2])
                            pr.op("scalar", lambda e, pq=pq, h=h, B=Ba: e.activation(
                                bwc(B), pq[:, 0:256], AF.Identity,
                                bias=bcolv(B_BQ + h), scale=1.0), waits=wsc)
                            f_aev[t * 8 + h] = pr.fence("scalar")
                            pr.op("scalar", lambda e, B=Ba, B2=Bb: e.activation(
                                bwc(B2), bwc(B), AF.Exp))
                        fs_e = pr.fence("scalar")
                        for h in (2 * p, 2 * p + 1):
                            Ba = BW0 if h % 2 == 0 else BW2
                            Bb = BW1 if h % 2 == 0 else BW3
                            pr.op("vector", lambda e, B2=Bb: e.tensor_scalar(
                                bwc(B2), bwc(B2), 1.0, None, ALU.min),
                                waits=[fs_e] if h % 2 == 0 else ())
                            pr.op("vector", lambda e, B=Ba: e.tensor_scalar(
                                bwc(B), bwc(B), 0.0, None, ALU.max))
                            pr.op("vector", lambda e, B=Ba, B2=Bb: e.tensor_tensor(
                                bwc(B), bwc(B2), bwc(B), ALU.add))
                            pr.op("vector", lambda e, h=h, B=Ba, B2=Bb:
                                  e.tensor_scalar(bwc(B2), bwc(B),
                                                  ccol(C_SUMK + h), 1e-6,
                                                  ALU.mult, ALU.add))
                            pr.op("vector", lambda e, h=h, B=Ba: e.tensor_scalar(
                                bwc(B), bwc(B), ccol(C_KV + h), None, ALU.mult))
                        fv_c = pr.fence("vector")
                        for h in (2 * p, 2 * p + 1):
                            Bb = BW1 if h % 2 == 0 else BW3
                            pr.op("scalar", lambda e, B2=Bb: e.activation(
                                bwc(B2), bwc(B2), AF.Ln),
                                waits=[fv_c] if h % 2 == 0 else ())
                            pr.op("scalar", lambda e, B2=Bb: e.activation(
                                bwc(B2), bwc(B2), AF.Exp, scale=-1.0))
                        fs_z = pr.fence("scalar")
                        for h in (2 * p, 2 * p + 1):
                            Ba = BW0 if h % 2 == 0 else BW2
                            Bb = BW1 if h % 2 == 0 else BW3
                            slot = HB_ROT0 if h % 2 == 0 else HB_ROT1
                            g2 = t * 8 + h - 2
                            wv2 = [fs_z] if h % 2 == 0 else []
                            if g2 >= 0:
                                wv2.append(f_wo[g2])
                            pr.op("vector", lambda e, slot=slot, B=Ba, B2=Bb:
                                  e.tensor_tensor(hbc(slot), bwc(B), bwc(B2),
                                                  ALU.mult), waits=wv2)
                            f_att[t * 8 + h] = pr.fence("vector")
                        for h in (2 * p, 2 * p + 1):
                            slot = HB_ROT0 if h % 2 == 0 else HB_ROT1
                            for oc in range(8):
                                pu = [PB2, PB3, PB4, PB5][oc // 2]
                                colr = slice((oc % 2) * 256, (oc % 2) * 256 + 256)
                                wt2 = []
                                if oc == 0:
                                    wt2 = [f_att[t * 8 + h]]
                                    if h == 0:
                                        wt2.append(f_evS)
                                pr.op("tensor", lambda e, pu=pu, colr=colr, R=Ra,
                                      h=h, oc=oc, slot=slot,
                                      st=(h == 0 and oc % 2 == 0):
                                      e.matmul(pu[:, colr], wo_t(R, h, oc),
                                               hbc(slot), start=st, stop=(h == 7)),
                                      waits=wt2)
                            f_wo[t * 8 + h] = pr.fence("tensor")
                        if t > 0:
                            if p == 0:
                                A_D1(t - 1)
                            elif p == 2:
                                A_D3(t - 1)
                    fmm = pr.fence("tensor")
                    fv_chain = pr.fence("vector")
                    for oc in range(8):
                        pu = [PB2, PB3, PB4, PB5][oc // 2]
                        colr = slice((oc % 2) * 256, (oc % 2) * 256 + 256)
                        bslot = BW4 if oc % 2 == 0 else BW5
                        wv3 = [fmm, fv_chain] if oc == 0 else []
                        if oc >= 2:
                            wv3 = [pr.fence("vector")]
                        pr.op("scalar", lambda e, pu=pu, colr=colr, oc=oc, b=bslot:
                              e.activation(bwc(b), pu[:, colr], AF.Gelu_apprx_tanh,
                                           bias=bcolv(B_BO + oc), scale=1.0),
                              waits=wv3)
                        fs = pr.fence("scalar")
                        pr.op("vector", lambda e, oc=oc, t=t, b=bslot:
                              e.tensor_tensor(schunk(oc, t * T, T),
                                              schunk(oc, t * T, T), bwc(b),
                                              ALU.add), waits=[fs])
                    f_evS = pr.fence("scalar")
                A_D1(NT - 1)
                A_D3(NT - 1)

                f_after_A = pr.fence("tensor")
                w_w2b = pr.dma("gpsimd", "wdW2b", lambda e, b=base, R=Ra: e.dma_start(
                    out=R[:, 16384:32768], in_=ap["wblob"][:, b + 81920:b + 98304]),
                    waits=[f_after_A])

                def M_D1(t):
                    emit_stats(PBX, lambda dc, t=t: schunk(dc, t * T, T))

                def M_D3(t):
                    emit_ln_finish(PBX, PBZ, EPS_SUB, (B_L2S, B_L2B, BIAS),
                                   lambda dc, t=t: schunk(dc, t * T, T),
                                   lambda dc, t=t: schunk(dc, t * T, T))

                f_h1 = {}
                f_out = {}
                f_evS2 = pr.fence("scalar")
                for t in range(NT):
                    cast_tile(t, waits=[pr.fence("tensor"), pr.fence("scalar")])
                    fcast = pr.fence("vector")
                    for fc in range(FC + 1):
                        if fc < FC:
                            ph = PB4 if fc % 2 == 0 else PB5
                            k2 = t * FC + fc - 2
                            wt = [fcast] if fc == 0 else []
                            if k2 >= 0:
                                wt.append(f_h1[k2])
                            elif t == 0 and fc < 2:
                                wt.append(pr.fence("scalar"))
                            for dc in range(DC):
                                pr.op("tensor", lambda e, ph=ph, R=Rb, dc=dc, fc=fc:
                                      e.matmul(ph[:, 0:256], w1_t(R, dc, fc),
                                               xtc(dc), start=(dc == 0),
                                               stop=(dc == DC - 1)),
                                      waits=(wt + ([w_w1] if (t == 0 and fc == 0) else [])) if dc == 0 else [])
                            fmm = pr.fence("tensor")
                            slot = HB_ROT0 if fc % 2 == 0 else HB_ROT1
                            k2o = t * FC + fc - 2
                            wsc = [fmm]
                            if k2o >= 0:
                                wsc.append(f_out[k2o])
                            pr.op("scalar", lambda e, ph=ph, fc=fc, slot=slot:
                                  e.activation(hbc(slot), ph[:, 0:256],
                                               AF.Gelu_apprx_tanh,
                                               bias=bcolv(B_B1 + fc), scale=1.0),
                                  waits=wsc)
                            f_h1[t * FC + fc] = pr.fence("scalar")
                        if fc >= 1:
                            f2 = fc - 1
                            slot = HB_ROT0 if f2 % 2 == 0 else HB_ROT1
                            for oc in range(8):
                                po = [PB0, PB1, PB2, PB3][oc // 2]
                                colr = slice((oc % 2) * 256, (oc % 2) * 256 + 256)
                                wt2 = []
                                if oc == 0:
                                    wt2 = [f_h1[t * FC + f2]]
                                    if f2 == 0:
                                        wt2.append(f_evS2)
                                pr.op("tensor", lambda e, po=po, colr=colr, R=Ra,
                                      f2=f2, oc=oc, slot=slot,
                                      st=(f2 == 0 and oc % 2 == 0):
                                      e.matmul(po[:, colr], w2_t(R, f2, oc),
                                               hbc(slot), start=st,
                                               stop=(f2 == FC - 1)),
                                      waits=wt2 + ([w_w2a] if (t == 0 and f2 == 0 and oc == 0) else []) +
                                            ([w_w2b] if (t == 0 and f2 == 16 and oc == 0) else []))
                            f_out[t * FC + f2] = pr.fence("tensor")
                        if t > 0:
                            if fc == 2:
                                M_D1(t - 1)
                            elif fc == 12:
                                M_D3(t - 1)
                    fmm = pr.fence("tensor")
                    fv_cast = pr.fence("vector")
                    fs_evac = pr.fence("scalar")
                    for oc in range(8):
                        po = [PB0, PB1, PB2, PB3][oc // 2]
                        colr = slice((oc % 2) * 256, (oc % 2) * 256 + 256)
                        bslot = BW0 if oc % 2 == 0 else BW3
                        wv3 = [fmm, fv_cast] if oc == 0 else []
                        if oc >= 2:
                            wv3 = [pr.fence("vector")]
                        pr.op("scalar", lambda e, po=po, colr=colr, oc=oc, b=bslot:
                              e.activation(bwc(b), po[:, colr], AF.Gelu_apprx_tanh,
                                           bias=bcolv(B_B2 + oc), scale=1.0),
                              waits=wv3)
                        fs = pr.fence("scalar")
                        pr.op("vector", lambda e, oc=oc, t=t, b=bslot:
                              e.tensor_tensor(schunk(oc, t * T, T),
                                              schunk(oc, t * T, T), bwc(b),
                                              ALU.add), waits=[fs])
                    f_evS2 = pr.fence("scalar")
                M_D1(NT - 1)
                M_D3(NT - 1)

                f_after_M = pr.fence("tensor")
                fs_end = pr.fence("scalar")
                fv_end = pr.fence("vector")
                for k in reg_free:
                    reg_free[k] = [f_after_M]
                bias_free = [fs_end, fv_end]

            if not skip_readd:
                for dc in range(DC):
                    for half in range(2):
                        lo = half * 1024
                        fv = pr.fence("vector")
                        fs = pr.fence("scalar")
                        wdx = pr.dma("sync", "xdR", lambda e, dc=dc, lo=lo:
                                     e.dma_start(out=BW[:, 0:1024],
                                                 in_=ap["ximg"][:, dc * S + lo: dc * S + lo + 1024]),
                                     waits=[fv, fs])
                        pr.op("vector", lambda e, dc=dc, lo=lo: e.tensor_tensor(
                            schunk(dc, lo, 1024), schunk(dc, lo, 1024),
                            BW[:, 0:1024], ALU.add), waits=[wdx])

        pr.lint()
        with nc.Block() as blk:
            pr.replay(blk, SEMS)

        if do_ffn:
            f_attn_t = pr.fence("tensor")
            from contextlib import ExitStack
            with ExitStack() as _stk2:
                PF0, PF1, PF2, PF3, PF4, PF5 = [
                    _stk2.enter_context(nc.psum_tensor(f"pf{j}", [P, 512], F32))
                    for j in range(6)]
                PT0 = _stk2.enter_context(nc.psum_tensor("pt0", [P, 1024], BF16))
                PT1 = _stk2.enter_context(nc.psum_tensor("pt1", [P, 1024], BF16))
                pr.op("vector", lambda e: e.memset(BW[0:1, BW4:BW4 + 128], 1.0),
                      waits=[f_attn_t])
                fvb = pr.fence("vector")
                pr.op("tensor", lambda e: e.matmul(
                    PF0[:, 0:1], BW[0:1, BW4:BW4 + 128],
                    FCONST[0:1, FC_SCALE:FC_SCALE + 1], start=True, stop=True),
                    waits=[fvb, w_fcb])
                fmm0 = pr.fence("tensor")
                pr.op("scalar", lambda e: e.activation(
                    ccol(C_SCL), PF0[:, 0:1], AF.Identity), waits=[fmm0])
                fg = pr.fence("scalar")
                for dc in range(DC):
                    pr.op("vector", lambda e, dc=dc: e.tensor_scalar(
                        ccol(C_SG + dc), FCONST[:, FC_SG + dc:FC_SG + dc + 1],
                        ccol(C_SCL), None, ALU.mult),
                        waits=[fg] if dc == 0 else ())
                    pr.op("vector", lambda e, dc=dc: e.tensor_scalar(
                        ccol(C_SB + dc), FCONST[:, FC_SB + dc:FC_SB + dc + 1],
                        ccol(C_SCL), None, ALU.mult))

                def emit_xn(XND):
                    for t in range(NT):
                        sb = PF2 if t % 2 == 0 else PF3
                        emit_stats(sb, lambda dc, t=t: schunk(dc, t * T, T))
                        emit_ln_finish(sb, PF4, EPS_TOP, (FC_NG, FC_NB, FCONST),
                                       lambda dc, t=t: schunk(dc, t * T, T),
                                       lambda dc, t=t, X=XND:
                                       X[:, dc * S + t * T: dc * S + t * T + T],
                                       double_eps2=EPS_FFN)

                def emit_se(SED, w_se):
                    for sl in range(4):
                        sb = PF2 if sl % 2 == 0 else PF3
                        emit_stats(sb,
                                   lambda dc, sl=sl, X=SED:
                                   X[:, dc * 1024 + sl * 256: dc * 1024 + sl * 256 + 256],
                                   wait_entry=[w_se] if sl == 0 else ())
                        emit_ln_finish(sb, PF4, EPS_FFN, (C_SG, C_SB, CONSTS),
                                       lambda dc, sl=sl, X=SED:
                                       X[:, dc * 1024 + sl * 256: dc * 1024 + sl * 256 + 256],
                                       lambda dc, sl=sl, X=SED:
                                       X[:, dc * 1024 + sl * 256: dc * 1024 + sl * 256 + 256])

                with nc.sbuf_tensor("slots", [P, 8192], BF16) as SLOTS:
                    with (
                        nc.sbuf_tensor("xn", [P, DC * S], BF16) as XN,
                        nc.sbuf_tensor("se", [P, 8192], BF16) as SE,
                        nc.sbuf_tensor("dspt", [P, 16384], BF16) as DSPT,
                        nc.sbuf_tensor("xnt", [P, 16384], BF16) as XNT,
                    ):
                        emit_xn(XN)
                        w_se = pr.dma("gpsimd", "wdF", lambda e: e.dma_start(
                            out=SE[:], in_=ap["fblob"][:, 0:8192]),
                            waits=[f_attn_t])
                        emit_se(SE, w_se)
                        f_xn_v = pr.fence("vector")
                        f_xn_s = pr.fence("scalar")
                        PFS = [PF0, PF1, PF2, PF3]
                        accs = [C_F1, C_F2, C_F3, C_F4]
                        for ec in range(8):
                            wt = ([f_xn_v, f_xn_s] if ec == 0 else
                                  [pr.fence("vector"), pr.fence("scalar")])
                            for sl4 in range(4):
                                for dc in range(DC):
                                    pr.op("tensor", lambda e, p=PFS[sl4], dc=dc,
                                          ec=ec, sl4=sl4: e.matmul(
                                              p[:, 0:512],
                                              SE[:, dc * 1024 + ec * 128: dc * 1024 + (ec + 1) * 128],
                                              XN[:, dc * S + sl4 * 512: dc * S + sl4 * 512 + 512],
                                              start=(dc == 0), stop=(dc == DC - 1)),
                                          waits=(wt if (dc == 0 and sl4 == 0) else []))
                            fmm = pr.fence("tensor")
                            pr.op("vector", lambda e: e.reduce_max(
                                ccol(C_F0), PF0[:, 0:512], axis=AX.X, negate=True),
                                waits=[fmm])
                            for sl4 in range(1, 4):
                                pr.op("vector", lambda e, p=PFS[sl4]: e.reduce_max(
                                    ccol(C_F1), p[:, 0:512], axis=AX.X, negate=True))
                                pr.op("vector", lambda e: e.tensor_scalar(
                                    ccol(C_F0), ccol(C_F1), ccol(C_F0), None,
                                    ALU.min))
                            fv = pr.fence("vector")
                            for sl4 in range(4):
                                pr.op("scalar", lambda e, p=PFS[sl4], sl4=sl4:
                                      e.activation(XT[:, sl4 * 512:(sl4 + 1) * 512],
                                                   p[:, 0:512], AF.Exp,
                                                   bias=ccol(C_F0), scale=1.0,
                                                   accum_out=ccol(accs[sl4])),
                                      waits=[fv] if sl4 == 0 else ())
                            fs = pr.fence("scalar")
                            pr.op("vector", lambda e: e.tensor_scalar(
                                ccol(C_F1), ccol(C_F2), ccol(C_F1), None, ALU.add),
                                waits=[fs])
                            pr.op("vector", lambda e: e.tensor_scalar(
                                ccol(C_F1), ccol(C_F3), ccol(C_F1), None, ALU.add))
                            pr.op("vector", lambda e: e.tensor_scalar(
                                ccol(C_F1), ccol(C_F4), ccol(C_F1), None, ALU.add))
                            pr.op("vector", lambda e: e.reciprocal(
                                ccol(C_F2), ccol(C_F1)))
                            pr.op("vector", lambda e: e.tensor_scalar(
                                XT[:], XT[:], ccol(C_F2), None, ALU.mult))
                            fv = pr.fence("vector")
                            f_tr = {}
                            for tc in range(16):
                                pt = PT0 if tc % 2 == 0 else PT1
                                wt3 = [fv] if tc == 0 else []
                                if tc >= 2:
                                    wt3.append(f_tr[tc - 2])
                                pr.op("tensor", lambda e, pt=pt, tc=tc: e.transpose(
                                    pt[:, 0:128], XT[:, tc * 128:(tc + 1) * 128],
                                    IDENT), waits=wt3)
                                fmm2 = pr.fence("tensor")
                                pr.op("vector", lambda e, pt=pt, tc=tc, ec=ec:
                                      e.tensor_copy(
                                          DSPT[:, tc * 1024 + ec * 128: tc * 1024 + (ec + 1) * 128],
                                          pt[:, 0:128]), waits=[fmm2])
                                f_tr[tc] = pr.fence("vector")
                        fv = pr.fence("vector")
                        f_tr = {}
                        for blk_i in range(128):
                            dc, tc = blk_i // 16, blk_i % 16
                            pt = PT0 if blk_i % 2 == 0 else PT1
                            wt3 = [fv] if blk_i == 0 else []
                            if blk_i >= 2:
                                wt3.append(f_tr[blk_i - 2])
                            pr.op("tensor", lambda e, pt=pt, dc=dc, tc=tc:
                                  e.transpose(pt[:, 0:128],
                                              XN[:, dc * S + tc * 128: dc * S + (tc + 1) * 128],
                                              IDENT), waits=wt3)
                            fmm2 = pr.fence("tensor")
                            pr.op("vector", lambda e, pt=pt, dc=dc, tc=tc:
                                  e.tensor_copy(
                                      XNT[:, tc * 1024 + dc * 128: tc * 1024 + (dc + 1) * 128],
                                      pt[:, 0:128]), waits=[fmm2])
                            f_tr[blk_i] = pr.fence("vector")
                        fv = pr.fence("vector")
                        f_sl = {}
                        for j in range(16):
                            dc, eh = j // 2, j % 2
                            pf = PF0 if j % 2 == 0 else PF1
                            wt4 = [fv] if j == 0 else []
                            if j >= 2:
                                wt4.append(f_sl[j - 2])
                            for tc in range(16):
                                pr.op("tensor", lambda e, pf=pf, dc=dc, eh=eh,
                                      tc=tc: e.matmul(
                                          pf[:, 0:512],
                                          XNT[:, tc * 1024 + dc * 128: tc * 1024 + (dc + 1) * 128],
                                          DSPT[:, tc * 1024 + eh * 512: tc * 1024 + eh * 512 + 512],
                                          start=(tc == 0), stop=(tc == 15)),
                                      waits=(wt4 if tc == 0 else []))
                            fmm2 = pr.fence("tensor")
                            pr.op("vector", lambda e, pf=pf, dc=dc, eh=eh:
                                  e.tensor_copy(
                                      SLOTS[:, dc * 1024 + eh * 512: dc * 1024 + eh * 512 + 512],
                                      pf[:, 0:512]), waits=[fmm2])
                            f_sl[j] = pr.fence("vector")
                    f_slots_t = pr.fence("tensor")
                    f_slots_v = pr.fence("vector")

                    _stkE = ExitStack()
                    EOUT = _stkE.enter_context(nc.sbuf_tensor("eout", [P, 8192], BF16))
                    with (
                        nc.sbuf_tensor("eh0", [P, 24576], BF16) as EH0,
                        nc.sbuf_tensor("eh1", [P, 24576], BF16) as EH1,
                    ):
                        EHS = [EH0, EH1]
                        eh_free = [[f_slots_t], [f_slots_t]]
                        fb_free = []
                        bd_free = [f_slots_v, pr.fence("scalar")]
                        f_act = {}
                        f_gu = {}
                        f_wd = {}
                        for ex in range(NEXP):
                            ebase = 8192 + ex * EXP_COLS
                            w_fb = pr.dma("sync", "xdFB", lambda e, eb=ebase:
                                          e.dma_start(out=FB[:],
                                                      in_=ap["fblob"][:, eb + 49152:eb + 49216]),
                                          waits=fb_free)
                            w_bd = pr.dma("sync", "xdBD", lambda e, ex=ex:
                                          e.dma_start(out=BW[0:SLOT, 0:1024],
                                                      in_=ap["bdimg"][:, ex * D:(ex + 1) * D]),
                                          waits=bd_free)
                            w_eh = [None, None]
                            for h in range(2):
                                si = (2 * ex + h) % 2
                                w_eh[h] = pr.dma("gpsimd", "wdE0" if si == 0 else "wdE1",
                                                 lambda e, eb=ebase, h=h, si=si:
                                                 e.dma_start(out=EHS[si][:],
                                                             in_=ap["fblob"][:, eb + h * 24576: eb + (h + 1) * 24576]),
                                                 waits=eh_free[si])
                            for h in range(2):
                                EH = EHS[(2 * ex + h) % 2]
                                for fq in range(8):
                                    g = 16 * ex + 8 * h + fq
                                    pg = PF2 if fq % 2 == 0 else PF3
                                    pe = PF4 if fq % 2 == 0 else PF5
                                    wt5 = [w_eh[h]] if fq == 0 else []
                                    if g >= 2:
                                        wt5.append(f_act[g - 2])
                                    if ex == 0 and h == 0 and fq == 0:
                                        wt5.append(f_slots_v)
                                    for dc in range(DC):
                                        pr.op("tensor", lambda e, pg=pg, EH=EH,
                                              dc=dc, fq=fq, ex=ex: e.matmul(
                                                  pg[:, 0:64],
                                                  EH[:, dc * 1024 + fq * 128: dc * 1024 + (fq + 1) * 128],
                                                  SLOTS[:, dc * 1024 + ex * 64: dc * 1024 + ex * 64 + 64],
                                                  start=(dc == 0),
                                                  stop=(dc == DC - 1)),
                                              waits=(wt5 if dc == 0 else []))
                                    for dc in range(DC):
                                        pr.op("tensor", lambda e, pg=pg, EH=EH,
                                              dc=dc, fq=fq, ex=ex: e.matmul(
                                                  pg[:, 64:128],
                                                  EH[:, 8192 + dc * 1024 + fq * 128: 8192 + dc * 1024 + (fq + 1) * 128],
                                                  SLOTS[:, dc * 1024 + ex * 64: dc * 1024 + ex * 64 + 64],
                                                  start=(dc == 0),
                                                  stop=(dc == DC - 1)))
                                    fmm3 = pr.fence("tensor")
                                    wsc = [fmm3]
                                    if h == 0 and fq == 0:
                                        wsc.append(w_fb)
                                    if g >= 2:
                                        wsc.append(f_gu[g - 2])
                                    pr.op("scalar", lambda e, pg=pg, pe=pe, h=h,
                                          fq=fq: e.activation(
                                              pe[:, 0:64], pg[:, 0:64], AF.Gelu,
                                              bias=FB[:, h * 8 + fq:h * 8 + fq + 1],
                                              scale=1.0), waits=wsc)
                                    ups = BW4 if fq % 2 == 0 else BW4 + 64
                                    pr.op("scalar", lambda e, pg=pg, h=h, fq=fq,
                                          u=ups: e.activation(
                                              BW[:, u:u + 64], pg[:, 64:128],
                                              AF.Identity,
                                              bias=FB[:, 16 + h * 8 + fq:16 + h * 8 + fq + 1],
                                              scale=1.0))
                                    f_act[g] = fs2 = pr.fence("scalar")
                                    gus = HB_GU0 if fq % 2 == 0 else HB_GU1
                                    wv5 = [fs2]
                                    if g >= 2:
                                        wv5.append(f_wd[g - 2])
                                    pr.op("vector", lambda e, pe=pe, gus=gus,
                                          u=ups: e.tensor_tensor(
                                              HB[:, gus:gus + 64], pe[:, 0:64],
                                              BW[:, u:u + 64], ALU.mult),
                                          waits=wv5)
                                    fv5 = pr.fence("vector")
                                    f_gu[g] = fv5
                                    prow = slice((ex % 2) * SLOT, (ex % 2) * SLOT + SLOT)
                                    for dh in range(2):
                                        pw = PF0 if dh == 0 else PF1
                                        wt6 = [fv5] if dh == 0 else []
                                        if h == 0 and fq == 0 and dh == 0 and ex > 0:
                                            wt6.append(pr.fence("vector"))
                                        pr.op("tensor", lambda e, pw=pw, EH=EH,
                                              fq=fq, dh=dh, gus=gus, prow=prow,
                                              st=(h == 0 and fq == 0):
                                              e.matmul(pw[prow, 0:512],
                                                       HB[:, gus:gus + 64],
                                                       EH[:, 16384 + fq * 1024 + dh * 512: 16384 + fq * 1024 + dh * 512 + 512],
                                                       start=st,
                                                       stop=(h == 1 and fq == 7)),
                                              waits=wt6)
                                    f_wd[g] = pr.fence("tensor")
                                eh_free[(2 * ex + h) % 2] = [pr.fence("tensor")]
                            fmm4 = pr.fence("tensor")
                            prow = slice((ex % 2) * SLOT, (ex % 2) * SLOT + SLOT)
                            for dh in range(2):
                                pw = PF0 if dh == 0 else PF1
                                pr.op("vector", lambda e, pw=pw, dh=dh, ex=ex,
                                      prow=prow: e.tensor_tensor(
                                          EOUT[prow, (ex // 2) * 1024 + dh * 512: (ex // 2) * 1024 + dh * 512 + 512],
                                          pw[prow, 0:512],
                                          BW[0:SLOT, dh * 512:dh * 512 + 512],
                                          ALU.add),
                                      waits=[fmm4, w_bd] if dh == 0 else [])
                            fb_free = [pr.fence("scalar")]
                            bd_free = [pr.fence("vector")]

                    with (
                        nc.sbuf_tensor("xn2", [P, DC * S], BF16) as XN2,
                        nc.sbuf_tensor("se2", [P, 8192], BF16) as SE2,
                    ):
                        emit_xn(XN2)
                        w_se2 = pr.dma("gpsimd", "wdF", lambda e: e.dma_start(
                            out=SE2[:], in_=ap["fblob"][:, 0:8192]),
                            waits=[pr.fence("tensor")])
                        emit_se(SE2, w_se2)
                        f_pre_v = pr.fence("vector")
                        f_pre_s = pr.fence("scalar")
                        for tc in range(16):
                            wt7 = ([f_pre_v, f_pre_s] if tc == 0 else
                                   [pr.fence("vector"), pr.fence("scalar")])
                            for dc in range(DC):
                                pr.op("tensor", lambda e, dc=dc, tc=tc: e.matmul(
                                    PF2[:, 0:512],
                                    XN2[:, dc * S + tc * 128: dc * S + (tc + 1) * 128],
                                    SE2[:, dc * 1024: dc * 1024 + 512],
                                    start=(dc == 0), stop=(dc == DC - 1)),
                                    waits=(wt7 if dc == 0 else []))
                                pr.op("tensor", lambda e, dc=dc, tc=tc: e.matmul(
                                    PF3[:, 0:512],
                                    XN2[:, dc * S + tc * 128: dc * S + (tc + 1) * 128],
                                    SE2[:, dc * 1024 + 512: dc * 1024 + 1024],
                                    start=(dc == 0), stop=(dc == DC - 1)))
                            fmm5 = pr.fence("tensor")
                            pr.op("vector", lambda e: e.reduce_max(
                                ccol(C_F0), PF2[:, 0:512], axis=AX.X, negate=True),
                                waits=[fmm5])
                            pr.op("vector", lambda e: e.reduce_max(
                                ccol(C_F1), PF3[:, 0:512], axis=AX.X, negate=True))
                            pr.op("vector", lambda e: e.tensor_scalar(
                                ccol(C_F0), ccol(C_F1), ccol(C_F0), None, ALU.min))
                            fv6 = pr.fence("vector")
                            pr.op("scalar", lambda e: e.activation(
                                XT[:, 0:512], PF2[:, 0:512], AF.Exp,
                                bias=ccol(C_F0), scale=1.0,
                                accum_out=ccol(C_F1)), waits=[fv6])
                            pr.op("scalar", lambda e: e.activation(
                                XT[:, 512:1024], PF3[:, 0:512], AF.Exp,
                                bias=ccol(C_F0), scale=1.0,
                                accum_out=ccol(C_F2)))
                            fs3 = pr.fence("scalar")
                            pr.op("vector", lambda e: e.tensor_scalar(
                                ccol(C_F1), ccol(C_F2), ccol(C_F1), None, ALU.add),
                                waits=[fs3])
                            pr.op("vector", lambda e: e.reciprocal(
                                ccol(C_F2), ccol(C_F1)))
                            pr.op("vector", lambda e: e.tensor_scalar(
                                XT[:, 0:1024], XT[:, 0:1024], ccol(C_F2), None,
                                ALU.mult))
                            fv6 = pr.fence("vector")
                            f_tr = {}
                            for esc in range(8):
                                pt = PT0 if esc % 2 == 0 else PT1
                                wt8 = [fv6] if esc == 0 else []
                                if esc >= 2:
                                    wt8.append(f_tr[esc - 2])
                                pr.op("tensor", lambda e, pt=pt, esc=esc:
                                      e.transpose(pt[:, 0:128],
                                                  XT[:, esc * 128:(esc + 1) * 128],
                                                  IDENT), waits=wt8)
                                fmm6 = pr.fence("tensor")
                                pr.op("vector", lambda e, pt=pt, esc=esc:
                                      e.tensor_copy(
                                          XT[:, 1024 + esc * 128: 1024 + (esc + 1) * 128],
                                          pt[:, 0:128]), waits=[fmm6])
                                f_tr[esc] = pr.fence("vector")
                            fv7 = pr.fence("vector")
                            for oc in range(8):
                                pw = PF4 if oc < 4 else PF5
                                colr = slice((oc % 4) * 128, (oc % 4) * 128 + 128)
                                for esc in range(8):
                                    pr.op("tensor", lambda e, pw=pw, colr=colr,
                                          oc=oc, esc=esc,
                                          st=(oc % 4 == 0 and esc == 0):
                                          e.matmul(pw[:, colr],
                                                   EOUT[:, esc * 1024 + oc * 128: esc * 1024 + (oc + 1) * 128],
                                                   XT[:, 1024 + esc * 128: 1024 + (esc + 1) * 128],
                                                   start=st, stop=(esc == 7)),
                                          waits=([fv7] if (esc == 0 and oc == 0) else []))
                            fmm7 = pr.fence("tensor")
                            for oc in range(8):
                                pw = PF4 if oc < 4 else PF5
                                colr = slice((oc % 4) * 128, (oc % 4) * 128 + 128)
                                pr.op("vector", lambda e, pw=pw, colr=colr, oc=oc,
                                      tc=tc: e.tensor_tensor(
                                          schunk(oc, tc * 128, 128),
                                          schunk(oc, tc * 128, 128),
                                          pw[:, colr], ALU.add),
                                      waits=[fmm7] if oc == 0 else [])

                    _stkE.close()

                fv = pr.fence("vector")
                fs = pr.fence("scalar")
                w_out = pr.dma("sync", "out", lambda e: e.dma_start(
                    out=ap["y"], in_=STREAM[:]), waits=[fv, fs])
                pr.op("sync", None, waits=[w_out])

                pr.lint()
                with nc.Block() as blk2:
                    pr.replay(blk2, SEMS)
        else:
            fv = pr.fence("vector")
            fs = pr.fence("scalar")
            w_out = pr.dma("sync", "out", lambda e: e.dma_start(
                out=ap["y"], in_=STREAM[:]), waits=[fv, fs])
            pr.op("sync", None, waits=[w_out])
            pr.lint()
            with nc.Block() as blk2:
                pr.replay(blk2, SEMS)

    return nc



def _img_fm(x_td):
    tok, feats = x_td.shape
    return np.ascontiguousarray(
        x_td.T.reshape(feats // P, P, tok).transpose(1, 0, 2).reshape(P, -1))


def _wimg(w, kchunks):
    K, N = w.shape
    assert K == kchunks * P
    return np.ascontiguousarray(
        w.reshape(kchunks, P, N).transpose(1, 0, 2).reshape(P, -1))


def _colimg(v):
    n = v.shape[0] // P
    return np.ascontiguousarray(v.reshape(n, P).T)


def prep_blobs(inp):
    f32 = np.float32
    blobs = {}
    segs = []
    for i in range(NSUB):
        wq = np.asarray(inp["L0_attn_Wq"][i], f32).transpose(1, 0, 2).reshape(D, D)
        wk = np.asarray(inp["L0_attn_Wk"][i], f32).transpose(1, 0, 2).reshape(D, D)
        wv = np.asarray(inp["L0_attn_Wv"][i], f32).transpose(1, 0, 2).reshape(D, D)
        wo = np.asarray(inp["L0_attn_Wo"][i], f32)
        w1 = np.asarray(inp["L0_attn_W1"][i], f32)
        w2 = np.asarray(inp["L0_attn_W2"][i], f32)
        segs += [_wimg(wk, DC), _wimg(wv, DC), _wimg(wq, DC), _wimg(wo, DC),
                 _wimg(w1, DC), _wimg(w2, FC)]
    blobs["wblob"] = np.concatenate(segs, axis=1)
    assert blobs["wblob"].shape == (P, WBLOB_COLS)
    bs = []
    for i in range(NSUB):
        img = np.zeros((P, 128), f32)
        img[:, B_BQ:B_BQ + 8] = _colimg(np.asarray(inp["L0_attn_bq"][i], f32).reshape(-1))
        img[:, B_BK:B_BK + 8] = _colimg(np.asarray(inp["L0_attn_bk"][i], f32).reshape(-1))
        img[:, B_BV:B_BV + 8] = _colimg(np.asarray(inp["L0_attn_bv"][i], f32).reshape(-1))
        img[:, B_BO:B_BO + 8] = _colimg(np.asarray(inp["L0_attn_bo"][i], f32))
        img[:, B_B1:B_B1 + 32] = _colimg(np.asarray(inp["L0_attn_b1"][i], f32))
        img[:, B_B2:B_B2 + 8] = _colimg(np.asarray(inp["L0_attn_b2"][i], f32))
        img[:, B_L1S:B_L1S + 8] = _colimg(np.asarray(inp["L0_attn_ln1_s"][i], f32))
        img[:, B_L1B:B_L1B + 8] = _colimg(np.asarray(inp["L0_attn_ln1_b"][i], f32))
        img[:, B_L2S:B_L2S + 8] = _colimg(np.asarray(inp["L0_attn_ln2_s"][i], f32))
        img[:, B_L2B:B_L2B + 8] = _colimg(np.asarray(inp["L0_attn_ln2_b"][i], f32))
        img[:, 104] = EPS_TOP
        img[:, 105] = EPS_SUB
        bs.append(img)
    blobs["bblob"] = np.concatenate(bs, axis=1)
    ib = np.zeros((P, 288), np.float32)
    ib[:, 0:128] = np.eye(P, dtype=np.float32)
    ib[:, IB_ONECOL] = 1
    ib[0, IB_ONEROW:IB_ONEROW + 128] = 1
    blobs["identb"] = ib
    se = np.asarray(inp["L0_ffn_slot_embeds"], f32).reshape(ES, D)
    fsegs = [_wimg(np.ascontiguousarray(se.T), DC)]
    for ex in range(NEXP):
        wg = np.asarray(inp["L0_ffn_Wg"][ex], f32)
        wu = np.asarray(inp["L0_ffn_Wu"][ex], f32)
        wd = np.asarray(inp["L0_ffn_Wd"][ex], f32)
        for h in range(2):
            fsegs.append(_wimg(np.ascontiguousarray(wg[:, h * 1024:(h + 1) * 1024]), DC))
            fsegs.append(_wimg(np.ascontiguousarray(wu[:, h * 1024:(h + 1) * 1024]), DC))
            fsegs.append(_wimg(np.ascontiguousarray(wd[h * 1024:(h + 1) * 1024, :]), DC))
        bimg = np.zeros((P, 64), f32)
        bimg[:, 0:16] = _colimg(np.asarray(inp["L0_ffn_bg"][ex], f32))
        bimg[:, 16:32] = _colimg(np.asarray(inp["L0_ffn_bu"][ex], f32))
        fsegs.append(bimg)
    blobs["fblob"] = np.concatenate(fsegs, axis=1)
    assert blobs["fblob"].shape == (P, FBLOB_COLS)
    fcb = np.zeros((P, 64), f32)
    fcb[:, FC_NG:FC_NG + 8] = _colimg(np.asarray(inp["L0_ffn_norm_g"], f32))
    fcb[:, FC_NB:FC_NB + 8] = _colimg(np.asarray(inp["L0_ffn_norm_b"], f32))
    fcb[:, FC_SG:FC_SG + 8] = _colimg(np.asarray(inp["L0_ffn_slot_g"], f32))
    fcb[:, FC_SB:FC_SB + 8] = _colimg(np.asarray(inp["L0_ffn_slot_b"], f32))
    fcb[0, FC_SCALE] = np.asarray(inp["L0_ffn_scale"], f32).reshape(-1)[0]
    blobs["fcb"] = fcb
    bd = np.asarray(inp["L0_ffn_bd"], f32)
    blobs["bdimg"] = np.ascontiguousarray(
        np.repeat(bd[:, None, :], SLOT, axis=1).transpose(1, 0, 2).reshape(SLOT, NEXP * D))
    return blobs


def kernel(**inputs):
    x = np.asarray(inputs["L0_attn_x"], np.float32)
    blobs = prep_blobs(inputs)
    nc = build_nc(NSUB, True)
    in_maps = []
    for c in range(8):
        m = dict(blobs)
        m["ximg"] = _img_fm(np.ascontiguousarray(x[c]))
        in_maps.append(m)
    res = run_bass_kernel_spmd(nc, in_maps, core_ids=list(range(8)))
    out = np.empty((8, S, D), np.float32)
    for c in range(8):
        yimg = np.asarray(res.results[c]["y"])
        out[c] = yimg.reshape(P, DC, S).transpose(1, 0, 2).reshape(D, S).T
    return out


# version 04
# speedup vs baseline: 1.2395x; 1.2395x over previous
import numpy as np
import ml_dtypes
import concourse.bass as bass
import concourse.mybir as mybir
from concourse.bass_utils import run_bass_kernel_spmd

F32 = mybir.dt.float32
BF16 = mybir.dt.bfloat16
AF = mybir.ActivationFunctionType
ALU = mybir.AluOpType
AX = mybir.AxisListType

P, D, S = 128, 1024, 2048
DC, T = 8, 256
NT = S // T
NSUB = 8
FC = 32
NEXP, SLOT, ES = 16, 64, 1024
EPS_TOP = 1e-5
EPS_SUB = 1e-6
EPS_FFN = 1e-5

WSUB = 98304
WBLOB_COLS = NSUB * WSUB
EXP_COLS = 6 * 8192 + 64
FBLOB_COLS = 8192 + NEXP * EXP_COLS

C_SUME, C_SUMR, C_SUMKV, C_SUMK, C_KV = 0, 8, 16, 24, 32
C_T0, C_T1 = 40, 41
C_SCL, C_SG, C_SB = 42, 43, 51
C_F0, C_F1, C_F2, C_F3, C_F4 = 0, 1, 2, 3, 4

B_BQ, B_BK, B_BV, B_BO, B_B1, B_B2 = 0, 8, 16, 24, 32, 64
B_L1S, B_L1B, B_L2S, B_L2B = 72, 80, 88, 96

FC_NG, FC_NB, FC_SG, FC_SB, FC_SCALE = 0, 8, 16, 24, 32

IB_ONECOL, IB_ONEROW = 128, 160

HB_ROT0, HB_ROT1 = 0, 256
HB_UB0 = 512
HB_UB1 = 1024
HB_RHI, HB_RLO, HB_BB = 1024, 1280, 1536
HB_GU0, HB_GU1 = 1792, 1856

BW0, BW1, BW2, BW3, BW4, BW5 = 0, 256, 512, 768, 1024, 1280


class Prog:
    ENG = ("tensor", "vector", "scalar", "gpsimd", "sync")

    def __init__(self):
        self.ops = []
        self.cnt = {e: 0 for e in self.ENG}
        for s in ("wdS", "wdA", "wdW1", "wdW2a", "wdW2b", "wdF", "wdE0",
                  "wdE1", "xdI", "xdB", "xdR", "xdFB", "xdBD", "out"):
            self.cnt[s] = 0
        self.pending = {e: None for e in self.ENG}
        self.cursor = 0
        self.lastwait = {e: {} for e in self.ENG}

    AUTO = ("vector", "scalar", "gpsimd")

    def op(self, eng, fn, waits=()):
        waits = [w for w in waits if w]
        if eng in self.AUTO:
            if self.cnt[eng] > 0:
                waits.append((eng, self.cnt[eng]))
            self.cnt[eng] += 1
            self.ops.append(dict(eng=eng, fn=fn, waits=waits, inc=(eng, 1)))
            self.pending[eng] = None
        else:
            self.ops.append(dict(eng=eng, fn=fn, waits=waits, inc=None))
            self.pending[eng] = len(self.ops) - 1

    def dma(self, eng, sem, fn, waits=()):
        self.cnt[sem] += 16
        self.ops.append(dict(eng=eng, fn=fn, waits=[w for w in waits if w],
                             inc=(sem, 16)))
        return (sem, self.cnt[sem])

    def fence(self, eng):
        if eng in self.AUTO:
            return (eng, self.cnt[eng])
        i = self.pending[eng]
        if i is not None:
            assert self.ops[i]["inc"] is None
            self.cnt[eng] += 1
            self.ops[i]["inc"] = (eng, 1)
            self.pending[eng] = None
        return (eng, self.cnt[eng])

    def lint(self):
        streams = {e: [] for e in self.ENG}
        for o in self.ops:
            streams[o["eng"]].append(o)
        pos = {e: 0 for e in self.ENG}
        counts = {k: 0 for k in self.cnt}
        progress = True
        while progress:
            progress = False
            for e in self.ENG:
                while pos[e] < len(streams[e]):
                    o = streams[e][pos[e]]
                    if all(counts[s] >= v for (s, v) in o["waits"]):
                        if o["inc"] is not None:
                            counts[o["inc"][0]] += o["inc"][1]
                        pos[e] += 1
                        progress = True
                    else:
                        break
        stuck = {e: (pos[e], len(streams[e])) for e in self.ENG
                 if pos[e] < len(streams[e])}
        if stuck:
            detail = {}
            for e in stuck:
                o = streams[e][pos[e]]
                detail[e] = [(s, v, counts[s]) for (s, v) in o["waits"]
                             if counts[s] < v]
            raise RuntimeError(f"sem deadlock: stuck={stuck} first-waits={detail}")

    class _TProxy:

        def __init__(self, eng):
            self._e = eng

        def matmul(self, *a, **k):
            k.setdefault("skip_group_check", True)
            return self._e.matmul(*a, **k)

        def __getattr__(self, n):
            return getattr(self._e, n)

    def replay(self, block, sems):
        todo = self.ops[self.cursor:]
        self.cursor = len(self.ops)
        for ename in self.ENG:
            myops = [o for o in todo if o["eng"] == ename]
            if not myops:
                continue
            sec = getattr(block, ename)

            def body(eng, myops=myops, ename=ename):
                if ename == "tensor":
                    eng = Prog._TProxy(eng)
                lw = self.lastwait[ename]
                for o in myops:
                    for (s, v) in o["waits"]:
                        if lw.get(s, -1) < v:
                            eng.wait_ge(sems[s], v)
                            lw[s] = v
                    if o["fn"] is not None:
                        instr = o["fn"](eng)
                        if o["inc"] is not None:
                            assert instr is not None
                            instr.then_inc(sems[o["inc"][0]], o["inc"][1])
                    else:
                        assert o["inc"] is None
            sec(body)


def build_nc(n_sub=NSUB, do_ffn=True, skip_readd=False):
    nc = bass.Bass("TRN2", target_bir_lowering=False)
    ap = {}
    ap["ximg"] = nc.dram_tensor("ximg", [P, DC * S], F32, kind="ExternalInput").ap()
    ap["wblob"] = nc.dram_tensor("wblob", [P, WBLOB_COLS], F32, kind="ExternalInput").ap()
    ap["bblob"] = nc.dram_tensor("bblob", [P, NSUB * 128], F32, kind="ExternalInput").ap()
    ap["identb"] = nc.dram_tensor("identb", [P, 288], F32, kind="ExternalInput").ap()
    if do_ffn:
        ap["fblob"] = nc.dram_tensor("fblob", [P, FBLOB_COLS], F32, kind="ExternalInput").ap()
        ap["fcb"] = nc.dram_tensor("fcb", [P, 64], F32, kind="ExternalInput").ap()
        ap["bdimg"] = nc.dram_tensor("bdimg", [SLOT, NEXP * D], F32, kind="ExternalInput").ap()
    ap["y"] = nc.dram_tensor("y", [P, DC * S], F32, kind="ExternalOutput").ap()

    pr = Prog()

    from contextlib import ExitStack
    with ExitStack() as _stk0:
        STREAM = _stk0.enter_context(nc.sbuf_tensor("stream", [P, DC * S], F32))
        XT = _stk0.enter_context(nc.sbuf_tensor("xt", [P, DC * T], BF16))
        BW = _stk0.enter_context(nc.sbuf_tensor("bw", [P, 1536], F32))
        HB = _stk0.enter_context(nc.sbuf_tensor("hb", [P, 1920], BF16))
        CONSTS = _stk0.enter_context(nc.sbuf_tensor("consts", [P, 64], F32))
        BIAS = _stk0.enter_context(nc.sbuf_tensor("biasb", [P, 128], F32))
        IDENTB = _stk0.enter_context(nc.sbuf_tensor("identb_sb", [P, 288], BF16))
        FCONST = _stk0.enter_context(nc.sbuf_tensor("fconst", [P, 64], F32))
        FB = _stk0.enter_context(nc.sbuf_tensor("fbias", [P, 64], F32))
        nc.const_aps.register(F32, float(EPS_TOP), BIAS[:, 104:105])
        nc.const_aps.register(F32, float(EPS_SUB), BIAS[:, 105:106])
        _sem_names = ["tensor", "vector", "scalar", "gpsimd", "sync",
                      "wdS", "wdA", "wdW1", "wdW2a", "wdW2b", "wdF", "wdE0",
                      "wdE1", "xdI", "xdB", "xdR", "xdFB", "xdBD", "out"]
        _sems = [_stk0.enter_context(nc.semaphore(f"sem_{n}")) for n in _sem_names]
        SEMS = dict(zip(_sem_names, _sems))

        def schunk(dc, lo, n):
            return STREAM[:, dc * S + lo: dc * S + lo + n]

        def xtc(dc):
            return XT[:, dc * T:(dc + 1) * T]

        def bwc(b, n=T):
            return BW[:, b:b + n]

        def hbc(b, n=T):
            return HB[:, b:b + n]

        def ccol(c):
            return CONSTS[:, c:c + 1]

        def bcolv(c):
            return BIAS[:, c:c + 1]

        ONES_COL = IDENTB[:, IB_ONECOL:IB_ONECOL + 1]
        ONES_ROW = IDENTB[0:1, IB_ONEROW:IB_ONEROW + 128]
        IDENT = IDENTB[:, 0:128]

        pr.dma("sync", "xdI", lambda e: e.dma_start(out=BW[:, 0:288], in_=ap["identb"]))
        pr.dma("sync", "xdI", lambda e: e.dma_start(out=BIAS[:], in_=ap["bblob"][:, 0:128]))
        if do_ffn:
            pr.dma("sync", "xdI", lambda e: e.dma_start(out=FCONST[:], in_=ap["fcb"]))
        w_init = pr.dma("sync", "xdI", lambda e: e.dma_start(out=STREAM[:], in_=ap["ximg"]))
        pr.op("vector", lambda e: e.tensor_copy(IDENTB[:], BW[:, 0:288]),
              waits=[w_init])
        w_ident = pr.fence("vector")
        w_bias0 = w_init
        w_fcb = w_init
        w_x = w_init

        ln_war = {"a": [], "b": []}

        def emit_stats(stat_bank, src_fn, wait_entry=()):
            f_mm = {}
            fv0 = pr.fence("vector")
            for dc in range(DC):
                src = src_fn(dc)
                ub = HB_UB0 if dc % 2 == 0 else HB_UB1
                if dc == 0:
                    ws = list(wait_entry) + [fv0] + ln_war["a"]
                elif dc == 1:
                    ws = list(ln_war["b"])
                else:
                    ws = [f_mm[dc - 2]]
                pr.op("scalar", lambda e, src=src, ub=ub: e.activation(
                    hbc(ub), src, AF.Identity), waits=ws)
                pr.op("scalar", lambda e, src=src, ub=ub: e.activation(
                    hbc(ub + 256), src, AF.Square))
                fs = pr.fence("scalar")
                pr.op("tensor", lambda e, b=stat_bank, ub=ub, f=(dc == 0),
                      l=(dc == DC - 1): e.matmul(
                          b[0:1, 0:512], ONES_COL, HB[:, ub:ub + 512],
                          start=f, stop=l), waits=[fs])
                f_mm[dc] = pr.fence("tensor")
            ln_war["a"] = [f_mm[DC - 2]]

        def emit_ln_finish(stat_bank, z_bank, eps, affine, src_fn, dst_fn,
                           double_eps2=None):
            ft = pr.fence("tensor")
            XA = stat_bank[0:1, 0:256]
            XB = stat_bank[0:1, 256:512]
            ZB = z_bank[0:1, 256:512]
            R = BW[0:1, BW4:BW4 + 256]

            pr.op("vector", lambda e: e.tensor_scalar(XA, XA, -1.0 / D, None, ALU.mult),
                  waits=[ft])
            pr.op("vector", lambda e: e.tensor_scalar(XB, XB, 1.0 / D, None, ALU.mult))
            fv = pr.fence("vector")
            pr.op("scalar", lambda e: e.activation(R, XA, AF.Square), waits=[fv])
            fs = pr.fence("scalar")
            pr.op("vector", lambda e: e.tensor_tensor(XB, XB, R, ALU.subtract),
                  waits=[fs])
            fv = pr.fence("vector")
            if double_eps2 is not None:
                pr.op("scalar", lambda e: e.activation(ZB, XB, AF.Identity), waits=[fv])
            pr.op("scalar", lambda e: e.activation(XB, XB, AF.Ln, bias=float(eps),
                                                   scale=1.0), waits=[fv])
            pr.op("scalar", lambda e: e.activation(XB, XB, AF.Exp, scale=-0.5))
            if double_eps2 is not None:
                pr.op("scalar", lambda e: e.activation(R, XB, AF.Square))
                fs = pr.fence("scalar")
                pr.op("vector", lambda e: e.tensor_tensor(ZB, ZB, R, ALU.mult),
                      waits=[fs])
                fv = pr.fence("vector")
                pr.op("scalar", lambda e: e.activation(ZB, ZB, AF.Ln,
                                                       bias=float(double_eps2),
                                                       scale=1.0), waits=[fv])
                pr.op("scalar", lambda e: e.activation(ZB, ZB, AF.Exp, scale=-0.5))
                pr.op("scalar", lambda e: e.activation(R, ZB, AF.Identity))
                fs = pr.fence("scalar")
                pr.op("vector", lambda e: e.tensor_tensor(XB, XB, R, ALU.mult),
                      waits=[fs])
            fv = pr.fence("vector")
            pr.op("scalar", lambda e: e.activation(R, XB, AF.Identity), waits=[fv])
            fs = pr.fence("scalar")
            pr.op("vector", lambda e: e.tensor_copy(HB[0:1, HB_RHI:HB_RHI + 256], R),
                  waits=[fs])
            ZA = z_bank[0:1, 0:256]
            pr.op("vector", lambda e: e.tensor_copy(ZA, HB[0:1, HB_RHI:HB_RHI + 256]))
            pr.op("vector", lambda e: e.tensor_tensor(ZA, R, ZA, ALU.subtract))
            pr.op("vector", lambda e: e.tensor_copy(HB[0:1, HB_RLO:HB_RLO + 256], ZA))
            pr.op("vector", lambda e: e.tensor_tensor(XA, XA, R, ALU.mult))
            pr.op("vector", lambda e: e.tensor_copy(HB[0:1, HB_BB:HB_BB + 256], XA))
            frow = pr.fence("vector")
            pr.op("tensor", lambda e, b=z_bank: e.matmul(
                b[:, 0:256], ONES_ROW, HB[0:1, HB_RHI:HB_RHI + 256],
                start=True, stop=False), waits=[frow])
            pr.op("tensor", lambda e, b=z_bank: e.matmul(
                b[:, 0:256], ONES_ROW, HB[0:1, HB_RLO:HB_RLO + 256],
                start=False, stop=True))
            pr.op("tensor", lambda e, b=z_bank: e.matmul(
                b[:, 256:512], ONES_ROW, HB[0:1, HB_BB:HB_BB + 256],
                start=False, stop=True))
            fb = pr.fence("tensor")
            ln_war["b"] = [fb]
            f_id = {}
            for dc in range(DC):
                src = src_fn(dc)
                dst = dst_fn(dc)
                slot = BW4 if dc % 2 == 0 else BW5
                wt_ap = [fb] if dc == 0 else []
                if dc >= 2:
                    wt_ap.append(f_id[dc - 2])
                pr.op("vector", lambda e, src=src, b=z_bank, sl=slot: e.tensor_tensor(
                    bwc(sl), src, b[:, 0:256], ALU.mult), waits=wt_ap)
                pr.op("vector", lambda e, b=z_bank, sl=slot: e.tensor_tensor(
                    bwc(sl), bwc(sl), b[:, 256:512], ALU.add))
                fv3 = pr.fence("vector")
                if affine is not None:
                    gb, bb2, ct = affine
                    pr.op("scalar", lambda e, dst=dst, gb=gb, bb2=bb2, ct=ct, dc=dc,
                          sl=slot: e.activation(dst, bwc(sl), AF.Identity,
                                                bias=ct[:, bb2 + dc:bb2 + dc + 1],
                                                scale=ct[:, gb + dc:gb + dc + 1]),
                          waits=[fv3])
                    f_id[dc] = pr.fence("scalar")
                else:
                    pr.op("scalar", lambda e, dst=dst, sl=slot: e.activation(
                        dst, bwc(sl), AF.Identity), waits=[fv3])
                    f_id[dc] = pr.fence("scalar")

        def cast_tile(t, waits=()):
            src = STREAM.ap().rearrange("p (c s) -> p c s", c=DC)[:, :, t * T:(t + 1) * T]
            dst = XT.ap().rearrange("p (c s) -> p c s", c=DC)
            pr.op("scalar", lambda e, src=src, dst=dst: e.activation(
                dst, src, AF.Identity), waits=waits)

        from contextlib import ExitStack
        with ExitStack() as _stk1:
            PB0, PB1, PB2, PB3, PB4, PB5, PB6, PB7 = [
                _stk1.enter_context(nc.psum_tensor(f"pb{j}", [P, 512], F32))
                for j in range(8)]
            WREG0 = _stk1.enter_context(nc.sbuf_tensor("wreg0", [P, 32768], BF16))
            WREG1 = _stk1.enter_context(nc.sbuf_tensor("wreg1", [P, 32768], BF16))
            PBX, PBZ = PB6, PB7

            for t in range(NT):
                emit_stats(PBX, lambda dc, t=t: schunk(dc, t * T, T),
                           wait_entry=[w_x] if t == 0 else ())
                emit_ln_finish(PBX, PBZ, EPS_TOP, None,
                               lambda dc, t=t: schunk(dc, t * T, T),
                               lambda dc, t=t: schunk(dc, t * T, T))

            WREGS = [WREG0, WREG1]

            def wk_t(R, dc, hc):
                return R[:, dc * 1024 + hc * 128: dc * 1024 + (hc + 1) * 128]

            def wv_t(R, dc, hc):
                return R[:, 8192 + dc * 1024 + hc * 128: 8192 + dc * 1024 + (hc + 1) * 128]

            def wq_t(R, dc, hc):
                return R[:, 16384 + dc * 1024 + hc * 128: 16384 + dc * 1024 + (hc + 1) * 128]

            def wo_t(R, hc, oc):
                return R[:, 24576 + hc * 1024 + oc * 128: 24576 + hc * 1024 + (oc + 1) * 128]

            def w1_t(R, dc, fc):
                return R[:, dc * 4096 + fc * 128: dc * 4096 + (fc + 1) * 128]

            def w2_t(R, fc, oc):
                return R[:, fc * 1024 + oc * 128: fc * 1024 + (oc + 1) * 128]

            reg_free = {(0, 0): [], (0, 1): [], (1, 0): [], (1, 1): []}
            bias_free = []

            for i in range(n_sub):
                Ra, Rb = WREGS[i % 2], WREGS[(i + 1) % 2]
                base = i * WSUB
                w_kv = pr.dma("gpsimd", "wdS", lambda e, b=base, R=Ra: e.dma_start(
                    out=R[:, 0:16384], in_=ap["wblob"][:, b:b + 16384]),
                    waits=reg_free[(i % 2, 0)])
                w_qo = pr.dma("gpsimd", "wdA", lambda e, b=base, R=Ra: e.dma_start(
                    out=R[:, 16384:32768], in_=ap["wblob"][:, b + 16384:b + 32768]),
                    waits=reg_free[(i % 2, 1)])
                pr.dma("gpsimd", "wdW1", lambda e, b=base, R=Rb: e.dma_start(
                    out=R[:, 0:16384], in_=ap["wblob"][:, b + 32768:b + 49152]),
                    waits=reg_free[((i + 1) % 2, 0)])
                w_w1 = pr.dma("gpsimd", "wdW1", lambda e, b=base, R=Rb: e.dma_start(
                    out=R[:, 16384:32768], in_=ap["wblob"][:, b + 49152:b + 65536]),
                    waits=reg_free[((i + 1) % 2, 1)])
                if i > 0:
                    w_bias = pr.dma("sync", "xdB", lambda e, i=i: e.dma_start(
                        out=BIAS[:], in_=ap["bblob"][:, i * 128:(i + 1) * 128]),
                        waits=bias_free)
                else:
                    w_bias = w_bias0

                f_er = {}
                f_pr = {}
                for t in range(NT):
                    cast_tile(t, waits=[pr.fence("tensor"), pr.fence("scalar")])
                    fcast = pr.fence("scalar")
                    for p in range(4):
                        for h in (2 * p, 2 * p + 1):
                            pk = PB0 if h % 2 == 0 else PB1
                            pv = PB2 if h % 2 == 0 else PB3
                            g2 = t * 8 + h - 2
                            wt = [fcast]
                            if g2 >= 0:
                                wt += [f_er[g2], f_pr[g2]]
                            for dc in range(DC):
                                pr.op("tensor", lambda e, pk=pk, R=Ra, dc=dc, h=h:
                                      e.matmul(pk[:, 0:256], wk_t(R, dc, h), xtc(dc),
                                               start=(dc == 0), stop=(dc == DC - 1)),
                                      waits=(wt + ([w_kv] if (t == 0 and h == 0) else [])) if dc == 0 else [])
                            for dc in range(DC):
                                pr.op("tensor", lambda e, pv=pv, R=Ra, dc=dc, h=h:
                                      e.matmul(pv[:, 0:256], wv_t(R, dc, h), xtc(dc),
                                               start=(dc == 0), stop=(dc == DC - 1)))
                        fmm = pr.fence("tensor")
                        for h in (2 * p, 2 * p + 1):
                            pk = PB0 if h % 2 == 0 else PB1
                            Ba = BW0 if h % 2 == 0 else BW2
                            Bb = BW1 if h % 2 == 0 else BW3
                            wsc = [fmm, w_bias]
                            pr.op("scalar", lambda e, pk=pk, h=h, B=Ba: e.activation(
                                bwc(B), pk[:, 0:256], AF.Exp,
                                bias=bcolv(B_BK + h), scale=1.0), waits=wsc)
                            pr.op("scalar", lambda e, pk=pk, h=h, B=Bb: e.activation(
                                bwc(B), pk[:, 0:256], AF.Relu,
                                bias=bcolv(B_BK + h), scale=1.0,
                                accum_out=ccol(C_T0 if h % 2 == 0 else C_T1)))
                            f_er[t * 8 + h] = pr.fence("scalar")
                        for h in (2 * p, 2 * p + 1):
                            pv = PB2 if h % 2 == 0 else PB3
                            Ba = BW0 if h % 2 == 0 else BW2
                            Bb = BW1 if h % 2 == 0 else BW3
                            ct = C_T0 if h % 2 == 0 else C_T1
                            first = (t == 0)
                            pr.op("vector", lambda e, h=h, B=Ba, first=first:
                                  e.tensor_scalar(bwc(B), bwc(B), 1.0,
                                                  None if first else ccol(C_SUME + h),
                                                  ALU.min, ALU.add,
                                                  accum_out=ccol(C_SUME + h)),
                                  waits=[f_er[t * 8 + h]])
                            pr.op("vector", lambda e, h=h, ct=ct, first=first:
                                  e.tensor_scalar(ccol(C_SUMR + h), ccol(ct),
                                                  0.0 if first else ccol(C_SUMR + h),
                                                  None, ALU.add))
                            pr.op("vector", lambda e, B=Ba, B2=Bb: e.tensor_tensor(
                                bwc(B), bwc(B), bwc(B2), ALU.add))
                            pr.op("vector", lambda e, pv=pv, B=Ba, B2=Bb:
                                  e.tensor_tensor(bwc(B2), bwc(B), pv[:, 0:256],
                                                  ALU.mult))
                            pr.op("vector", lambda e, B2=Bb, ct=ct: e.reduce_sum(
                                ccol(ct), bwc(B2), axis=AX.X))
                            pr.op("vector", lambda e, h=h, ct=ct, first=first:
                                  e.tensor_scalar(ccol(C_SUMKV + h), ccol(ct),
                                                  0.0 if first else ccol(C_SUMKV + h),
                                                  None, ALU.add))
                            f_pr[t * 8 + h] = pr.fence("vector")
                for hc in range(8):
                    pr.op("vector", lambda e, hc=hc: e.tensor_scalar(
                        ccol(C_SUMK + hc), ccol(C_SUME + hc),
                        ccol(C_SUMR + hc), None, ALU.add),
                        waits=[w_bias] if hc == 0 else ())
                    pr.op("vector", lambda e, hc=hc: e.tensor_scalar(
                        ccol(C_T0), ccol(C_SUMK + hc), bcolv(B_BV + hc),
                        None, ALU.mult))
                    pr.op("vector", lambda e, hc=hc: e.tensor_scalar(
                        ccol(C_KV + hc), ccol(C_T0), ccol(C_SUMKV + hc),
                        None, ALU.add))
                f_after_S = pr.fence("tensor")
                f_S_v = pr.fence("vector")
                w_w2a = pr.dma("gpsimd", "wdW2a", lambda e, b=base, R=Ra: e.dma_start(
                    out=R[:, 0:16384], in_=ap["wblob"][:, b + 65536:b + 81920]),
                    waits=[f_after_S])

                def A_D1(t):
                    emit_stats(PBX, lambda dc, t=t: schunk(dc, t * T, T))

                def A_D3(t):
                    emit_ln_finish(PBX, PBZ, EPS_SUB, (B_L1S, B_L1B, BIAS),
                                   lambda dc, t=t: schunk(dc, t * T, T),
                                   lambda dc, t=t: schunk(dc, t * T, T))

                f_aev = {}
                f_att = {}
                f_wo = {}
                f_evS = pr.fence("scalar")
                for t in range(NT):
                    cast_tile(t, waits=[pr.fence("tensor"), pr.fence("scalar")])
                    fcast = pr.fence("scalar")
                    for p in range(4):
                        for h in (2 * p, 2 * p + 1):
                            pq = PB0 if h % 2 == 0 else PB1
                            g2 = t * 8 + h - 2
                            wt = [fcast] + ([f_aev[g2]] if g2 >= 0 else
                                            [pr.fence("scalar")])
                            for dc in range(DC):
                                pr.op("tensor", lambda e, pq=pq, R=Ra, dc=dc, h=h:
                                      e.matmul(pq[:, 0:256], wq_t(R, dc, h),
                                               xtc(dc), start=(dc == 0),
                                               stop=(dc == DC - 1)),
                                      waits=(wt + ([w_qo] if (t == 0 and h == 0) else [])) if dc == 0 else [])
                        fmm = pr.fence("tensor")
                        for h in (2 * p, 2 * p + 1):
                            pq = PB0 if h % 2 == 0 else PB1
                            Ba = BW0 if h % 2 == 0 else BW2
                            Bb = BW1 if h % 2 == 0 else BW3
                            g2 = t * 8 + h - 2
                            wsc = [fmm]
                            if g2 >= 0:
                                wsc.append(f_att[g2])
                            else:
                                wsc.append(f_S_v)
                            pr.op("scalar", lambda e, pq=pq, h=h, B=Ba: e.activation(
                                bwc(B), pq[:, 0:256], AF.Identity,
                                bias=bcolv(B_BQ + h), scale=1.0), waits=wsc)
                            f_aev[t * 8 + h] = pr.fence("scalar")
                            pr.op("scalar", lambda e, B=Ba, B2=Bb: e.activation(
                                bwc(B2), bwc(B), AF.Exp))
                        fs_e = pr.fence("scalar")
                        for h in (2 * p, 2 * p + 1):
                            Ba = BW0 if h % 2 == 0 else BW2
                            Bb = BW1 if h % 2 == 0 else BW3
                            pr.op("vector", lambda e, B2=Bb: e.tensor_scalar(
                                bwc(B2), bwc(B2), 1.0, None, ALU.min),
                                waits=[fs_e] if h % 2 == 0 else ())
                            pr.op("vector", lambda e, B=Ba: e.tensor_scalar(
                                bwc(B), bwc(B), 0.0, None, ALU.max))
                            pr.op("vector", lambda e, B=Ba, B2=Bb: e.tensor_tensor(
                                bwc(B), bwc(B2), bwc(B), ALU.add))
                            pr.op("vector", lambda e, h=h, B=Ba, B2=Bb:
                                  e.tensor_scalar(bwc(B2), bwc(B),
                                                  ccol(C_SUMK + h), 1e-6,
                                                  ALU.mult, ALU.add))
                            pr.op("vector", lambda e, h=h, B=Ba: e.tensor_scalar(
                                bwc(B), bwc(B), ccol(C_KV + h), None, ALU.mult))
                        fv_c = pr.fence("vector")
                        for h in (2 * p, 2 * p + 1):
                            Bb = BW1 if h % 2 == 0 else BW3
                            pr.op("scalar", lambda e, B2=Bb: e.activation(
                                bwc(B2), bwc(B2), AF.Ln),
                                waits=[fv_c] if h % 2 == 0 else ())
                            pr.op("scalar", lambda e, B2=Bb: e.activation(
                                bwc(B2), bwc(B2), AF.Exp, scale=-1.0))
                        fs_z = pr.fence("scalar")
                        for h in (2 * p, 2 * p + 1):
                            Ba = BW0 if h % 2 == 0 else BW2
                            Bb = BW1 if h % 2 == 0 else BW3
                            slot = HB_ROT0 if h % 2 == 0 else HB_ROT1
                            g2 = t * 8 + h - 2
                            wv2 = [fs_z] if h % 2 == 0 else []
                            if g2 >= 0:
                                wv2.append(f_wo[g2])
                            pr.op("vector", lambda e, slot=slot, B=Ba, B2=Bb:
                                  e.tensor_tensor(hbc(slot), bwc(B), bwc(B2),
                                                  ALU.mult), waits=wv2)
                            f_att[t * 8 + h] = pr.fence("vector")
                        for h in (2 * p, 2 * p + 1):
                            slot = HB_ROT0 if h % 2 == 0 else HB_ROT1
                            for oc in range(8):
                                pu = [PB2, PB3, PB4, PB5][oc // 2]
                                colr = slice((oc % 2) * 256, (oc % 2) * 256 + 256)
                                wt2 = []
                                if oc == 0:
                                    wt2 = [f_att[t * 8 + h]]
                                    if h == 0:
                                        wt2.append(f_evS)
                                pr.op("tensor", lambda e, pu=pu, colr=colr, R=Ra,
                                      h=h, oc=oc, slot=slot,
                                      st=(h == 0 and oc % 2 == 0):
                                      e.matmul(pu[:, colr], wo_t(R, h, oc),
                                               hbc(slot), start=st, stop=(h == 7)),
                                      waits=wt2)
                            f_wo[t * 8 + h] = pr.fence("tensor")
                        if t > 0:
                            if p == 0:
                                A_D1(t - 1)
                            elif p == 2:
                                A_D3(t - 1)
                    fmm = pr.fence("tensor")
                    fv_chain = pr.fence("vector")
                    for oc in range(8):
                        pu = [PB2, PB3, PB4, PB5][oc // 2]
                        colr = slice((oc % 2) * 256, (oc % 2) * 256 + 256)
                        bslot = BW4 if oc % 2 == 0 else BW5
                        wv3 = [fmm, fv_chain] if oc == 0 else []
                        if oc >= 2:
                            wv3 = [pr.fence("vector")]
                        pr.op("scalar", lambda e, pu=pu, colr=colr, oc=oc, b=bslot:
                              e.activation(bwc(b), pu[:, colr], AF.Gelu_apprx_tanh,
                                           bias=bcolv(B_BO + oc), scale=1.0),
                              waits=wv3)
                        fs = pr.fence("scalar")
                        pr.op("vector", lambda e, oc=oc, t=t, b=bslot:
                              e.tensor_tensor(schunk(oc, t * T, T),
                                              schunk(oc, t * T, T), bwc(b),
                                              ALU.add), waits=[fs])
                    f_evS = pr.fence("scalar")
                A_D1(NT - 1)
                A_D3(NT - 1)

                f_after_A = pr.fence("tensor")
                w_w2b = pr.dma("gpsimd", "wdW2b", lambda e, b=base, R=Ra: e.dma_start(
                    out=R[:, 16384:32768], in_=ap["wblob"][:, b + 81920:b + 98304]),
                    waits=[f_after_A])

                def M_D1(t):
                    emit_stats(PBX, lambda dc, t=t: schunk(dc, t * T, T))

                def M_D3(t):
                    emit_ln_finish(PBX, PBZ, EPS_SUB, (B_L2S, B_L2B, BIAS),
                                   lambda dc, t=t: schunk(dc, t * T, T),
                                   lambda dc, t=t: schunk(dc, t * T, T))

                f_h1 = {}
                f_out = {}
                f_evS2 = pr.fence("scalar")
                for t in range(NT):
                    cast_tile(t, waits=[pr.fence("tensor"), pr.fence("scalar")])
                    fcast = pr.fence("scalar")
                    for fc in range(FC + 1):
                        if fc < FC:
                            ph = PB4 if fc % 2 == 0 else PB5
                            k2 = t * FC + fc - 2
                            wt = [fcast] if fc == 0 else []
                            if k2 >= 0:
                                wt.append(f_h1[k2])
                            elif t == 0 and fc < 2:
                                wt.append(pr.fence("scalar"))
                            for dc in range(DC):
                                pr.op("tensor", lambda e, ph=ph, R=Rb, dc=dc, fc=fc:
                                      e.matmul(ph[:, 0:256], w1_t(R, dc, fc),
                                               xtc(dc), start=(dc == 0),
                                               stop=(dc == DC - 1)),
                                      waits=(wt + ([w_w1] if (t == 0 and fc == 0) else [])) if dc == 0 else [])
                            fmm = pr.fence("tensor")
                            slot = HB_ROT0 if fc % 2 == 0 else HB_ROT1
                            k2o = t * FC + fc - 2
                            wsc = [fmm]
                            if k2o >= 0:
                                wsc.append(f_out[k2o])
                            pr.op("scalar", lambda e, ph=ph, fc=fc, slot=slot:
                                  e.activation(hbc(slot), ph[:, 0:256],
                                               AF.Gelu_apprx_tanh,
                                               bias=bcolv(B_B1 + fc), scale=1.0),
                                  waits=wsc)
                            f_h1[t * FC + fc] = pr.fence("scalar")
                        if fc >= 1:
                            f2 = fc - 1
                            slot = HB_ROT0 if f2 % 2 == 0 else HB_ROT1
                            for oc in range(8):
                                po = [PB0, PB1, PB2, PB3][oc // 2]
                                colr = slice((oc % 2) * 256, (oc % 2) * 256 + 256)
                                wt2 = []
                                if oc == 0:
                                    wt2 = [f_h1[t * FC + f2]]
                                    if f2 == 0:
                                        wt2.append(f_evS2)
                                pr.op("tensor", lambda e, po=po, colr=colr, R=Ra,
                                      f2=f2, oc=oc, slot=slot,
                                      st=(f2 == 0 and oc % 2 == 0):
                                      e.matmul(po[:, colr], w2_t(R, f2, oc),
                                               hbc(slot), start=st,
                                               stop=(f2 == FC - 1)),
                                      waits=wt2 + ([w_w2a] if (t == 0 and f2 == 0 and oc == 0) else []) +
                                            ([w_w2b] if (t == 0 and f2 == 16 and oc == 0) else []))
                            f_out[t * FC + f2] = pr.fence("tensor")
                        if t > 0:
                            if fc == 2:
                                M_D1(t - 1)
                            elif fc == 12:
                                M_D3(t - 1)
                    fmm = pr.fence("tensor")
                    fv_cast = pr.fence("vector")
                    fs_evac = pr.fence("scalar")
                    for oc in range(8):
                        po = [PB0, PB1, PB2, PB3][oc // 2]
                        colr = slice((oc % 2) * 256, (oc % 2) * 256 + 256)
                        bslot = BW0 if oc % 2 == 0 else BW3
                        wv3 = [fmm, fv_cast] if oc == 0 else []
                        if oc >= 2:
                            wv3 = [pr.fence("vector")]
                        pr.op("scalar", lambda e, po=po, colr=colr, oc=oc, b=bslot:
                              e.activation(bwc(b), po[:, colr], AF.Gelu_apprx_tanh,
                                           bias=bcolv(B_B2 + oc), scale=1.0),
                              waits=wv3)
                        fs = pr.fence("scalar")
                        pr.op("vector", lambda e, oc=oc, t=t, b=bslot:
                              e.tensor_tensor(schunk(oc, t * T, T),
                                              schunk(oc, t * T, T), bwc(b),
                                              ALU.add), waits=[fs])
                    f_evS2 = pr.fence("scalar")
                M_D1(NT - 1)
                M_D3(NT - 1)

                f_after_M = pr.fence("tensor")
                fs_end = pr.fence("scalar")
                fv_end = pr.fence("vector")
                for k in reg_free:
                    reg_free[k] = [f_after_M]
                bias_free = [fs_end, fv_end]

            if not skip_readd:
                for dc in range(DC):
                    for half in range(2):
                        lo = half * 1024
                        fv = pr.fence("vector")
                        fs = pr.fence("scalar")
                        wdx = pr.dma("sync", "xdR", lambda e, dc=dc, lo=lo:
                                     e.dma_start(out=BW[:, 0:1024],
                                                 in_=ap["ximg"][:, dc * S + lo: dc * S + lo + 1024]),
                                     waits=[fv, fs])
                        pr.op("vector", lambda e, dc=dc, lo=lo: e.tensor_tensor(
                            schunk(dc, lo, 1024), schunk(dc, lo, 1024),
                            BW[:, 0:1024], ALU.add), waits=[wdx])

        pr.lint()
        with nc.Block() as blk:
            pr.replay(blk, SEMS)

        if do_ffn:
            f_attn_t = pr.fence("tensor")
            from contextlib import ExitStack
            with ExitStack() as _stk2:
                PF0, PF1, PF2, PF3, PF4, PF5 = [
                    _stk2.enter_context(nc.psum_tensor(f"pf{j}", [P, 512], F32))
                    for j in range(6)]
                PT0 = _stk2.enter_context(nc.psum_tensor("pt0", [P, 1024], BF16))
                PT1 = _stk2.enter_context(nc.psum_tensor("pt1", [P, 1024], BF16))
                pr.op("vector", lambda e: e.memset(BW[0:1, BW4:BW4 + 128], 1.0),
                      waits=[f_attn_t])
                fvb = pr.fence("vector")
                pr.op("tensor", lambda e: e.matmul(
                    PF0[:, 0:1], BW[0:1, BW4:BW4 + 128],
                    FCONST[0:1, FC_SCALE:FC_SCALE + 1], start=True, stop=True),
                    waits=[fvb, w_fcb])
                fmm0 = pr.fence("tensor")
                pr.op("scalar", lambda e: e.activation(
                    ccol(C_SCL), PF0[:, 0:1], AF.Identity), waits=[fmm0])
                fg = pr.fence("scalar")
                for dc in range(DC):
                    pr.op("vector", lambda e, dc=dc: e.tensor_scalar(
                        ccol(C_SG + dc), FCONST[:, FC_SG + dc:FC_SG + dc + 1],
                        ccol(C_SCL), None, ALU.mult),
                        waits=[fg] if dc == 0 else ())
                    pr.op("vector", lambda e, dc=dc: e.tensor_scalar(
                        ccol(C_SB + dc), FCONST[:, FC_SB + dc:FC_SB + dc + 1],
                        ccol(C_SCL), None, ALU.mult))

                def emit_xn(XND):
                    for t in range(NT):
                        sb = PF2 if t % 2 == 0 else PF3
                        emit_stats(sb, lambda dc, t=t: schunk(dc, t * T, T))
                        emit_ln_finish(sb, PF4, EPS_TOP, (FC_NG, FC_NB, FCONST),
                                       lambda dc, t=t: schunk(dc, t * T, T),
                                       lambda dc, t=t, X=XND:
                                       X[:, dc * S + t * T: dc * S + t * T + T],
                                       double_eps2=EPS_FFN)

                def emit_se(SED, w_se):
                    for sl in range(4):
                        sb = PF2 if sl % 2 == 0 else PF3
                        emit_stats(sb,
                                   lambda dc, sl=sl, X=SED:
                                   X[:, dc * 1024 + sl * 256: dc * 1024 + sl * 256 + 256],
                                   wait_entry=[w_se] if sl == 0 else ())
                        emit_ln_finish(sb, PF4, EPS_FFN, (C_SG, C_SB, CONSTS),
                                       lambda dc, sl=sl, X=SED:
                                       X[:, dc * 1024 + sl * 256: dc * 1024 + sl * 256 + 256],
                                       lambda dc, sl=sl, X=SED:
                                       X[:, dc * 1024 + sl * 256: dc * 1024 + sl * 256 + 256])

                with nc.sbuf_tensor("slots", [P, 8192], BF16) as SLOTS:
                    with (
                        nc.sbuf_tensor("xn", [P, DC * S], BF16) as XN,
                        nc.sbuf_tensor("se", [P, 8192], BF16) as SE,
                        nc.sbuf_tensor("dspt", [P, 16384], BF16) as DSPT,
                        nc.sbuf_tensor("xnt", [P, 16384], BF16) as XNT,
                    ):
                        emit_xn(XN)
                        w_se = pr.dma("gpsimd", "wdF", lambda e: e.dma_start(
                            out=SE[:], in_=ap["fblob"][:, 0:8192]),
                            waits=[f_attn_t])
                        emit_se(SE, w_se)
                        f_xn_v = pr.fence("vector")
                        f_xn_s = pr.fence("scalar")
                        PFS = [PF0, PF1, PF2, PF3]
                        accs = [C_F1, C_F2, C_F3, C_F4]
                        for ec in range(8):
                            wt = ([f_xn_v, f_xn_s] if ec == 0 else
                                  [pr.fence("vector"), pr.fence("scalar")])
                            for sl4 in range(4):
                                for dc in range(DC):
                                    pr.op("tensor", lambda e, p=PFS[sl4], dc=dc,
                                          ec=ec, sl4=sl4: e.matmul(
                                              p[:, 0:512],
                                              SE[:, dc * 1024 + ec * 128: dc * 1024 + (ec + 1) * 128],
                                              XN[:, dc * S + sl4 * 512: dc * S + sl4 * 512 + 512],
                                              start=(dc == 0), stop=(dc == DC - 1)),
                                          waits=(wt if (dc == 0 and sl4 == 0) else []))
                            fmm = pr.fence("tensor")
                            pr.op("vector", lambda e: e.reduce_max(
                                ccol(C_F0), PF0[:, 0:512], axis=AX.X, negate=True),
                                waits=[fmm])
                            for sl4 in range(1, 4):
                                pr.op("vector", lambda e, p=PFS[sl4]: e.reduce_max(
                                    ccol(C_F1), p[:, 0:512], axis=AX.X, negate=True))
                                pr.op("vector", lambda e: e.tensor_scalar(
                                    ccol(C_F0), ccol(C_F1), ccol(C_F0), None,
                                    ALU.min))
                            fv = pr.fence("vector")
                            for sl4 in range(4):
                                pr.op("scalar", lambda e, p=PFS[sl4], sl4=sl4:
                                      e.activation(XT[:, sl4 * 512:(sl4 + 1) * 512],
                                                   p[:, 0:512], AF.Exp,
                                                   bias=ccol(C_F0), scale=1.0,
                                                   accum_out=ccol(accs[sl4])),
                                      waits=[fv] if sl4 == 0 else ())
                            fs = pr.fence("scalar")
                            pr.op("vector", lambda e: e.tensor_scalar(
                                ccol(C_F1), ccol(C_F2), ccol(C_F1), None, ALU.add),
                                waits=[fs])
                            pr.op("vector", lambda e: e.tensor_scalar(
                                ccol(C_F1), ccol(C_F3), ccol(C_F1), None, ALU.add))
                            pr.op("vector", lambda e: e.tensor_scalar(
                                ccol(C_F1), ccol(C_F4), ccol(C_F1), None, ALU.add))
                            pr.op("vector", lambda e: e.reciprocal(
                                ccol(C_F2), ccol(C_F1)))
                            pr.op("vector", lambda e: e.tensor_scalar(
                                XT[:], XT[:], ccol(C_F2), None, ALU.mult))
                            fv = pr.fence("vector")
                            f_tr = {}
                            for tc in range(16):
                                pt = PT0 if tc % 2 == 0 else PT1
                                wt3 = [fv] if tc == 0 else []
                                if tc >= 2:
                                    wt3.append(f_tr[tc - 2])
                                pr.op("tensor", lambda e, pt=pt, tc=tc: e.transpose(
                                    pt[:, 0:128], XT[:, tc * 128:(tc + 1) * 128],
                                    IDENT), waits=wt3)
                                fmm2 = pr.fence("tensor")
                                pr.op("vector", lambda e, pt=pt, tc=tc, ec=ec:
                                      e.tensor_copy(
                                          DSPT[:, tc * 1024 + ec * 128: tc * 1024 + (ec + 1) * 128],
                                          pt[:, 0:128]), waits=[fmm2])
                                f_tr[tc] = pr.fence("vector")
                        fv = pr.fence("vector")
                        f_tr = {}
                        for blk_i in range(128):
                            dc, tc = blk_i // 16, blk_i % 16
                            pt = PT0 if blk_i % 2 == 0 else PT1
                            wt3 = [fv] if blk_i == 0 else []
                            if blk_i >= 2:
                                wt3.append(f_tr[blk_i - 2])
                            pr.op("tensor", lambda e, pt=pt, dc=dc, tc=tc:
                                  e.transpose(pt[:, 0:128],
                                              XN[:, dc * S + tc * 128: dc * S + (tc + 1) * 128],
                                              IDENT), waits=wt3)
                            fmm2 = pr.fence("tensor")
                            pr.op("vector", lambda e, pt=pt, dc=dc, tc=tc:
                                  e.tensor_copy(
                                      XNT[:, tc * 1024 + dc * 128: tc * 1024 + (dc + 1) * 128],
                                      pt[:, 0:128]), waits=[fmm2])
                            f_tr[blk_i] = pr.fence("vector")
                        fv = pr.fence("vector")
                        f_sl = {}
                        for j in range(16):
                            dc, eh = j // 2, j % 2
                            pf = PF0 if j % 2 == 0 else PF1
                            wt4 = [fv] if j == 0 else []
                            if j >= 2:
                                wt4.append(f_sl[j - 2])
                            for tc in range(16):
                                pr.op("tensor", lambda e, pf=pf, dc=dc, eh=eh,
                                      tc=tc: e.matmul(
                                          pf[:, 0:512],
                                          XNT[:, tc * 1024 + dc * 128: tc * 1024 + (dc + 1) * 128],
                                          DSPT[:, tc * 1024 + eh * 512: tc * 1024 + eh * 512 + 512],
                                          start=(tc == 0), stop=(tc == 15)),
                                      waits=(wt4 if tc == 0 else []))
                            fmm2 = pr.fence("tensor")
                            pr.op("vector", lambda e, pf=pf, dc=dc, eh=eh:
                                  e.tensor_copy(
                                      SLOTS[:, dc * 1024 + eh * 512: dc * 1024 + eh * 512 + 512],
                                      pf[:, 0:512]), waits=[fmm2])
                            f_sl[j] = pr.fence("vector")
                    f_slots_t = pr.fence("tensor")
                    f_slots_v = pr.fence("vector")

                    _stkE = ExitStack()
                    EOUT = _stkE.enter_context(nc.sbuf_tensor("eout", [P, 8192], BF16))
                    with (
                        nc.sbuf_tensor("eh0", [P, 24576], BF16) as EH0,
                        nc.sbuf_tensor("eh1", [P, 24576], BF16) as EH1,
                    ):
                        EHS = [EH0, EH1]
                        eh_free = [[f_slots_t], [f_slots_t]]
                        fb_free = []
                        bd_free = [f_slots_v, pr.fence("scalar")]
                        f_act = {}
                        f_gu = {}
                        f_wd = {}
                        for ex in range(NEXP):
                            ebase = 8192 + ex * EXP_COLS
                            w_fb = pr.dma("sync", "xdFB", lambda e, eb=ebase:
                                          e.dma_start(out=FB[:],
                                                      in_=ap["fblob"][:, eb + 49152:eb + 49216]),
                                          waits=fb_free)
                            w_bd = pr.dma("sync", "xdBD", lambda e, ex=ex:
                                          e.dma_start(out=BW[0:SLOT, 0:1024],
                                                      in_=ap["bdimg"][:, ex * D:(ex + 1) * D]),
                                          waits=bd_free)
                            w_eh = [None, None]
                            for h in range(2):
                                si = (2 * ex + h) % 2
                                w_eh[h] = pr.dma("gpsimd", "wdE0" if si == 0 else "wdE1",
                                                 lambda e, eb=ebase, h=h, si=si:
                                                 e.dma_start(out=EHS[si][:],
                                                             in_=ap["fblob"][:, eb + h * 24576: eb + (h + 1) * 24576]),
                                                 waits=eh_free[si])
                            for h in range(2):
                                EH = EHS[(2 * ex + h) % 2]
                                for fq in range(8):
                                    g = 16 * ex + 8 * h + fq
                                    pg = PF2 if fq % 2 == 0 else PF3
                                    pe = PF4 if fq % 2 == 0 else PF5
                                    wt5 = [w_eh[h]] if fq == 0 else []
                                    if g >= 2:
                                        wt5.append(f_act[g - 2])
                                    if ex == 0 and h == 0 and fq == 0:
                                        wt5.append(f_slots_v)
                                    for dc in range(DC):
                                        pr.op("tensor", lambda e, pg=pg, EH=EH,
                                              dc=dc, fq=fq, ex=ex: e.matmul(
                                                  pg[:, 0:64],
                                                  EH[:, dc * 1024 + fq * 128: dc * 1024 + (fq + 1) * 128],
                                                  SLOTS[:, dc * 1024 + ex * 64: dc * 1024 + ex * 64 + 64],
                                                  start=(dc == 0),
                                                  stop=(dc == DC - 1)),
                                              waits=(wt5 if dc == 0 else []))
                                    for dc in range(DC):
                                        pr.op("tensor", lambda e, pg=pg, EH=EH,
                                              dc=dc, fq=fq, ex=ex: e.matmul(
                                                  pg[:, 64:128],
                                                  EH[:, 8192 + dc * 1024 + fq * 128: 8192 + dc * 1024 + (fq + 1) * 128],
                                                  SLOTS[:, dc * 1024 + ex * 64: dc * 1024 + ex * 64 + 64],
                                                  start=(dc == 0),
                                                  stop=(dc == DC - 1)))
                                    fmm3 = pr.fence("tensor")
                                    wsc = [fmm3]
                                    if h == 0 and fq == 0:
                                        wsc.append(w_fb)
                                    if g >= 2:
                                        wsc.append(f_gu[g - 2])
                                    pr.op("scalar", lambda e, pg=pg, pe=pe, h=h,
                                          fq=fq: e.activation(
                                              pe[:, 0:64], pg[:, 0:64], AF.Gelu,
                                              bias=FB[:, h * 8 + fq:h * 8 + fq + 1],
                                              scale=1.0), waits=wsc)
                                    ups = BW4 if fq % 2 == 0 else BW4 + 64
                                    pr.op("scalar", lambda e, pg=pg, h=h, fq=fq,
                                          u=ups: e.activation(
                                              BW[:, u:u + 64], pg[:, 64:128],
                                              AF.Identity,
                                              bias=FB[:, 16 + h * 8 + fq:16 + h * 8 + fq + 1],
                                              scale=1.0))
                                    f_act[g] = fs2 = pr.fence("scalar")
                                    gus = HB_GU0 if fq % 2 == 0 else HB_GU1
                                    wv5 = [fs2]
                                    if g >= 2:
                                        wv5.append(f_wd[g - 2])
                                    pr.op("vector", lambda e, pe=pe, gus=gus,
                                          u=ups: e.tensor_tensor(
                                              HB[:, gus:gus + 64], pe[:, 0:64],
                                              BW[:, u:u + 64], ALU.mult),
                                          waits=wv5)
                                    fv5 = pr.fence("vector")
                                    f_gu[g] = fv5
                                    prow = slice((ex % 2) * SLOT, (ex % 2) * SLOT + SLOT)
                                    for dh in range(2):
                                        pw = PF0 if dh == 0 else PF1
                                        wt6 = [fv5] if dh == 0 else []
                                        if h == 0 and fq == 0 and dh == 0 and ex > 0:
                                            wt6.append(pr.fence("vector"))
                                        pr.op("tensor", lambda e, pw=pw, EH=EH,
                                              fq=fq, dh=dh, gus=gus, prow=prow,
                                              st=(h == 0 and fq == 0):
                                              e.matmul(pw[prow, 0:512],
                                                       HB[:, gus:gus + 64],
                                                       EH[:, 16384 + fq * 1024 + dh * 512: 16384 + fq * 1024 + dh * 512 + 512],
                                                       start=st,
                                                       stop=(h == 1 and fq == 7)),
                                              waits=wt6)
                                    f_wd[g] = pr.fence("tensor")
                                eh_free[(2 * ex + h) % 2] = [pr.fence("tensor")]
                            fmm4 = pr.fence("tensor")
                            prow = slice((ex % 2) * SLOT, (ex % 2) * SLOT + SLOT)
                            for dh in range(2):
                                pw = PF0 if dh == 0 else PF1
                                pr.op("vector", lambda e, pw=pw, dh=dh, ex=ex,
                                      prow=prow: e.tensor_tensor(
                                          EOUT[prow, (ex // 2) * 1024 + dh * 512: (ex // 2) * 1024 + dh * 512 + 512],
                                          pw[prow, 0:512],
                                          BW[0:SLOT, dh * 512:dh * 512 + 512],
                                          ALU.add),
                                      waits=[fmm4, w_bd] if dh == 0 else [])
                            fb_free = [pr.fence("scalar")]
                            bd_free = [pr.fence("vector")]

                    with (
                        nc.sbuf_tensor("xn2", [P, DC * S], BF16) as XN2,
                        nc.sbuf_tensor("se2", [P, 8192], BF16) as SE2,
                    ):
                        emit_xn(XN2)
                        w_se2 = pr.dma("gpsimd", "wdF", lambda e: e.dma_start(
                            out=SE2[:], in_=ap["fblob"][:, 0:8192]),
                            waits=[pr.fence("tensor")])
                        emit_se(SE2, w_se2)
                        f_pre_v = pr.fence("vector")
                        f_pre_s = pr.fence("scalar")
                        for tc in range(16):
                            wt7 = ([f_pre_v, f_pre_s] if tc == 0 else
                                   [pr.fence("vector"), pr.fence("scalar")])
                            for dc in range(DC):
                                pr.op("tensor", lambda e, dc=dc, tc=tc: e.matmul(
                                    PF2[:, 0:512],
                                    XN2[:, dc * S + tc * 128: dc * S + (tc + 1) * 128],
                                    SE2[:, dc * 1024: dc * 1024 + 512],
                                    start=(dc == 0), stop=(dc == DC - 1)),
                                    waits=(wt7 if dc == 0 else []))
                                pr.op("tensor", lambda e, dc=dc, tc=tc: e.matmul(
                                    PF3[:, 0:512],
                                    XN2[:, dc * S + tc * 128: dc * S + (tc + 1) * 128],
                                    SE2[:, dc * 1024 + 512: dc * 1024 + 1024],
                                    start=(dc == 0), stop=(dc == DC - 1)))
                            fmm5 = pr.fence("tensor")
                            pr.op("vector", lambda e: e.reduce_max(
                                ccol(C_F0), PF2[:, 0:512], axis=AX.X, negate=True),
                                waits=[fmm5])
                            pr.op("vector", lambda e: e.reduce_max(
                                ccol(C_F1), PF3[:, 0:512], axis=AX.X, negate=True))
                            pr.op("vector", lambda e: e.tensor_scalar(
                                ccol(C_F0), ccol(C_F1), ccol(C_F0), None, ALU.min))
                            fv6 = pr.fence("vector")
                            pr.op("scalar", lambda e: e.activation(
                                XT[:, 0:512], PF2[:, 0:512], AF.Exp,
                                bias=ccol(C_F0), scale=1.0,
                                accum_out=ccol(C_F1)), waits=[fv6])
                            pr.op("scalar", lambda e: e.activation(
                                XT[:, 512:1024], PF3[:, 0:512], AF.Exp,
                                bias=ccol(C_F0), scale=1.0,
                                accum_out=ccol(C_F2)))
                            fs3 = pr.fence("scalar")
                            pr.op("vector", lambda e: e.tensor_scalar(
                                ccol(C_F1), ccol(C_F2), ccol(C_F1), None, ALU.add),
                                waits=[fs3])
                            pr.op("vector", lambda e: e.reciprocal(
                                ccol(C_F2), ccol(C_F1)))
                            pr.op("vector", lambda e: e.tensor_scalar(
                                XT[:, 0:1024], XT[:, 0:1024], ccol(C_F2), None,
                                ALU.mult))
                            fv6 = pr.fence("vector")
                            f_tr = {}
                            for esc in range(8):
                                pt = PT0 if esc % 2 == 0 else PT1
                                wt8 = [fv6] if esc == 0 else []
                                if esc >= 2:
                                    wt8.append(f_tr[esc - 2])
                                pr.op("tensor", lambda e, pt=pt, esc=esc:
                                      e.transpose(pt[:, 0:128],
                                                  XT[:, esc * 128:(esc + 1) * 128],
                                                  IDENT), waits=wt8)
                                fmm6 = pr.fence("tensor")
                                pr.op("vector", lambda e, pt=pt, esc=esc:
                                      e.tensor_copy(
                                          XT[:, 1024 + esc * 128: 1024 + (esc + 1) * 128],
                                          pt[:, 0:128]), waits=[fmm6])
                                f_tr[esc] = pr.fence("vector")
                            fv7 = pr.fence("vector")
                            for oc in range(8):
                                pw = PF4 if oc < 4 else PF5
                                colr = slice((oc % 4) * 128, (oc % 4) * 128 + 128)
                                for esc in range(8):
                                    pr.op("tensor", lambda e, pw=pw, colr=colr,
                                          oc=oc, esc=esc,
                                          st=(oc % 4 == 0 and esc == 0):
                                          e.matmul(pw[:, colr],
                                                   EOUT[:, esc * 1024 + oc * 128: esc * 1024 + (oc + 1) * 128],
                                                   XT[:, 1024 + esc * 128: 1024 + (esc + 1) * 128],
                                                   start=st, stop=(esc == 7)),
                                          waits=([fv7] if (esc == 0 and oc == 0) else []))
                            fmm7 = pr.fence("tensor")
                            for oc in range(8):
                                pw = PF4 if oc < 4 else PF5
                                colr = slice((oc % 4) * 128, (oc % 4) * 128 + 128)
                                pr.op("vector", lambda e, pw=pw, colr=colr, oc=oc,
                                      tc=tc: e.tensor_tensor(
                                          schunk(oc, tc * 128, 128),
                                          schunk(oc, tc * 128, 128),
                                          pw[:, colr], ALU.add),
                                      waits=[fmm7] if oc == 0 else [])

                    _stkE.close()

                fv = pr.fence("vector")
                fs = pr.fence("scalar")
                w_out = pr.dma("sync", "out", lambda e: e.dma_start(
                    out=ap["y"], in_=STREAM[:]), waits=[fv, fs])
                pr.op("sync", None, waits=[w_out])

                pr.lint()
                with nc.Block() as blk2:
                    pr.replay(blk2, SEMS)
        else:
            fv = pr.fence("vector")
            fs = pr.fence("scalar")
            w_out = pr.dma("sync", "out", lambda e: e.dma_start(
                out=ap["y"], in_=STREAM[:]), waits=[fv, fs])
            pr.op("sync", None, waits=[w_out])
            pr.lint()
            with nc.Block() as blk2:
                pr.replay(blk2, SEMS)

    return nc



def _img_fm(x_td):
    tok, feats = x_td.shape
    return np.ascontiguousarray(
        x_td.T.reshape(feats // P, P, tok).transpose(1, 0, 2).reshape(P, -1))


def _wimg(w, kchunks):
    K, N = w.shape
    assert K == kchunks * P
    return np.ascontiguousarray(
        w.reshape(kchunks, P, N).transpose(1, 0, 2).reshape(P, -1))


def _colimg(v):
    n = v.shape[0] // P
    return np.ascontiguousarray(v.reshape(n, P).T)


def prep_blobs(inp):
    f32 = np.float32
    blobs = {}
    segs = []
    for i in range(NSUB):
        wq = np.asarray(inp["L0_attn_Wq"][i], f32).transpose(1, 0, 2).reshape(D, D)
        wk = np.asarray(inp["L0_attn_Wk"][i], f32).transpose(1, 0, 2).reshape(D, D)
        wv = np.asarray(inp["L0_attn_Wv"][i], f32).transpose(1, 0, 2).reshape(D, D)
        wo = np.asarray(inp["L0_attn_Wo"][i], f32)
        w1 = np.asarray(inp["L0_attn_W1"][i], f32)
        w2 = np.asarray(inp["L0_attn_W2"][i], f32)
        segs += [_wimg(wk, DC), _wimg(wv, DC), _wimg(wq, DC), _wimg(wo, DC),
                 _wimg(w1, DC), _wimg(w2, FC)]
    blobs["wblob"] = np.concatenate(segs, axis=1)
    assert blobs["wblob"].shape == (P, WBLOB_COLS)
    bs = []
    for i in range(NSUB):
        img = np.zeros((P, 128), f32)
        img[:, B_BQ:B_BQ + 8] = _colimg(np.asarray(inp["L0_attn_bq"][i], f32).reshape(-1))
        img[:, B_BK:B_BK + 8] = _colimg(np.asarray(inp["L0_attn_bk"][i], f32).reshape(-1))
        img[:, B_BV:B_BV + 8] = _colimg(np.asarray(inp["L0_attn_bv"][i], f32).reshape(-1))
        img[:, B_BO:B_BO + 8] = _colimg(np.asarray(inp["L0_attn_bo"][i], f32))
        img[:, B_B1:B_B1 + 32] = _colimg(np.asarray(inp["L0_attn_b1"][i], f32))
        img[:, B_B2:B_B2 + 8] = _colimg(np.asarray(inp["L0_attn_b2"][i], f32))
        img[:, B_L1S:B_L1S + 8] = _colimg(np.asarray(inp["L0_attn_ln1_s"][i], f32))
        img[:, B_L1B:B_L1B + 8] = _colimg(np.asarray(inp["L0_attn_ln1_b"][i], f32))
        img[:, B_L2S:B_L2S + 8] = _colimg(np.asarray(inp["L0_attn_ln2_s"][i], f32))
        img[:, B_L2B:B_L2B + 8] = _colimg(np.asarray(inp["L0_attn_ln2_b"][i], f32))
        img[:, 104] = EPS_TOP
        img[:, 105] = EPS_SUB
        bs.append(img)
    blobs["bblob"] = np.concatenate(bs, axis=1)
    ib = np.zeros((P, 288), np.float32)
    ib[:, 0:128] = np.eye(P, dtype=np.float32)
    ib[:, IB_ONECOL] = 1
    ib[0, IB_ONEROW:IB_ONEROW + 128] = 1
    blobs["identb"] = ib
    se = np.asarray(inp["L0_ffn_slot_embeds"], f32).reshape(ES, D)
    fsegs = [_wimg(np.ascontiguousarray(se.T), DC)]
    for ex in range(NEXP):
        wg = np.asarray(inp["L0_ffn_Wg"][ex], f32)
        wu = np.asarray(inp["L0_ffn_Wu"][ex], f32)
        wd = np.asarray(inp["L0_ffn_Wd"][ex], f32)
        for h in range(2):
            fsegs.append(_wimg(np.ascontiguousarray(wg[:, h * 1024:(h + 1) * 1024]), DC))
            fsegs.append(_wimg(np.ascontiguousarray(wu[:, h * 1024:(h + 1) * 1024]), DC))
            fsegs.append(_wimg(np.ascontiguousarray(wd[h * 1024:(h + 1) * 1024, :]), DC))
        bimg = np.zeros((P, 64), f32)
        bimg[:, 0:16] = _colimg(np.asarray(inp["L0_ffn_bg"][ex], f32))
        bimg[:, 16:32] = _colimg(np.asarray(inp["L0_ffn_bu"][ex], f32))
        fsegs.append(bimg)
    blobs["fblob"] = np.concatenate(fsegs, axis=1)
    assert blobs["fblob"].shape == (P, FBLOB_COLS)
    fcb = np.zeros((P, 64), f32)
    fcb[:, FC_NG:FC_NG + 8] = _colimg(np.asarray(inp["L0_ffn_norm_g"], f32))
    fcb[:, FC_NB:FC_NB + 8] = _colimg(np.asarray(inp["L0_ffn_norm_b"], f32))
    fcb[:, FC_SG:FC_SG + 8] = _colimg(np.asarray(inp["L0_ffn_slot_g"], f32))
    fcb[:, FC_SB:FC_SB + 8] = _colimg(np.asarray(inp["L0_ffn_slot_b"], f32))
    fcb[0, FC_SCALE] = np.asarray(inp["L0_ffn_scale"], f32).reshape(-1)[0]
    blobs["fcb"] = fcb
    bd = np.asarray(inp["L0_ffn_bd"], f32)
    blobs["bdimg"] = np.ascontiguousarray(
        np.repeat(bd[:, None, :], SLOT, axis=1).transpose(1, 0, 2).reshape(SLOT, NEXP * D))
    return blobs


def kernel(**inputs):
    x = np.asarray(inputs["L0_attn_x"], np.float32)
    blobs = prep_blobs(inputs)
    nc = build_nc(NSUB, True)
    in_maps = []
    for c in range(8):
        m = dict(blobs)
        m["ximg"] = _img_fm(np.ascontiguousarray(x[c]))
        in_maps.append(m)
    res = run_bass_kernel_spmd(nc, in_maps, core_ids=list(range(8)))
    out = np.empty((8, S, D), np.float32)
    for c in range(8):
        yimg = np.asarray(res.results[c]["y"])
        out[c] = yimg.reshape(P, DC, S).transpose(1, 0, 2).reshape(D, S).T
    return out


# version 05 (final)
# speedup vs baseline: 1.2910x; 1.0416x over previous
import numpy as np
import ml_dtypes
import concourse.bass as bass
import concourse.mybir as mybir
from concourse.bass_utils import run_bass_kernel_spmd

F32 = mybir.dt.float32
BF16 = mybir.dt.bfloat16
AF = mybir.ActivationFunctionType
ALU = mybir.AluOpType
AX = mybir.AxisListType

P, D, S = 128, 1024, 2048
DC, T = 8, 256
NT = S // T
NSUB = 8
FC = 32
NEXP, SLOT, ES = 16, 64, 1024
EPS_TOP = 1e-5
EPS_SUB = 1e-6
EPS_FFN = 1e-5

WSUB = 98304
WBLOB_COLS = NSUB * WSUB
EXP_COLS = 6 * 8192 + 64
FBLOB_COLS = 8192 + NEXP * EXP_COLS

C_SUME, C_SUMR, C_SUMKV, C_SUMK, C_KV = 0, 8, 16, 24, 32
C_T0, C_T1 = 40, 41
C_SCL, C_SG, C_SB = 42, 43, 51
C_F0, C_F1, C_F2, C_F3, C_F4 = 0, 1, 2, 3, 4

B_BQ, B_BK, B_BV, B_BO, B_B1, B_B2 = 0, 8, 16, 24, 32, 64
B_L1S, B_L1B, B_L2S, B_L2B = 72, 80, 88, 96

FC_NG, FC_NB, FC_SG, FC_SB, FC_SCALE = 0, 8, 16, 24, 32

IB_ONECOL, IB_ONEROW = 128, 160

HB_ROT0, HB_ROT1 = 0, 256
HB_UB0 = 512
HB_UB1 = 1024
HB_RHI, HB_RLO, HB_BB = 1024, 1280, 1536
HB_GU0, HB_GU1 = 1792, 1856

BW0, BW1, BW2, BW3, BW4, BW5 = 0, 256, 512, 768, 1024, 1280


class Prog:
    ENG = ("tensor", "vector", "scalar", "gpsimd", "sync")

    def __init__(self):
        self.ops = []
        self.cnt = {e: 0 for e in self.ENG}
        for s in ("wdS", "wdA", "wdW1", "wdW2a", "wdW2b", "wdF", "wdE0",
                  "wdE1", "xdI", "xdB", "xdR", "xdFB", "xdBD", "out"):
            self.cnt[s] = 0
        self.pending = {e: None for e in self.ENG}
        self.cursor = 0
        self.lastwait = {e: {} for e in self.ENG}

    AUTO = ("vector", "scalar", "gpsimd")

    def op(self, eng, fn, waits=()):
        waits = [w for w in waits if w]
        if eng in self.AUTO:
            if self.cnt[eng] > 0:
                waits.append((eng, self.cnt[eng]))
            self.cnt[eng] += 1
            self.ops.append(dict(eng=eng, fn=fn, waits=waits, inc=(eng, 1)))
            self.pending[eng] = None
        else:
            self.ops.append(dict(eng=eng, fn=fn, waits=waits, inc=None))
            self.pending[eng] = len(self.ops) - 1

    def dma(self, eng, sem, fn, waits=()):
        self.cnt[sem] += 16
        self.ops.append(dict(eng=eng, fn=fn, waits=[w for w in waits if w],
                             inc=(sem, 16)))
        return (sem, self.cnt[sem])

    def fence(self, eng):
        if eng in self.AUTO:
            return (eng, self.cnt[eng])
        i = self.pending[eng]
        if i is not None:
            assert self.ops[i]["inc"] is None
            self.cnt[eng] += 1
            self.ops[i]["inc"] = (eng, 1)
            self.pending[eng] = None
        return (eng, self.cnt[eng])

    def lint(self):
        streams = {e: [] for e in self.ENG}
        for o in self.ops:
            streams[o["eng"]].append(o)
        pos = {e: 0 for e in self.ENG}
        counts = {k: 0 for k in self.cnt}
        progress = True
        while progress:
            progress = False
            for e in self.ENG:
                while pos[e] < len(streams[e]):
                    o = streams[e][pos[e]]
                    if all(counts[s] >= v for (s, v) in o["waits"]):
                        if o["inc"] is not None:
                            counts[o["inc"][0]] += o["inc"][1]
                        pos[e] += 1
                        progress = True
                    else:
                        break
        stuck = {e: (pos[e], len(streams[e])) for e in self.ENG
                 if pos[e] < len(streams[e])}
        if stuck:
            detail = {}
            for e in stuck:
                o = streams[e][pos[e]]
                detail[e] = [(s, v, counts[s]) for (s, v) in o["waits"]
                             if counts[s] < v]
            raise RuntimeError(f"sem deadlock: stuck={stuck} first-waits={detail}")

    class _TProxy:

        def __init__(self, eng):
            self._e = eng

        def matmul(self, *a, **k):
            k.setdefault("skip_group_check", True)
            return self._e.matmul(*a, **k)

        def __getattr__(self, n):
            return getattr(self._e, n)

    def replay(self, block, sems):
        todo = self.ops[self.cursor:]
        self.cursor = len(self.ops)
        for ename in self.ENG:
            myops = [o for o in todo if o["eng"] == ename]
            if not myops:
                continue
            sec = getattr(block, ename)

            def body(eng, myops=myops, ename=ename):
                if ename == "tensor":
                    eng = Prog._TProxy(eng)
                lw = self.lastwait[ename]
                for o in myops:
                    for (s, v) in o["waits"]:
                        if lw.get(s, -1) < v:
                            eng.wait_ge(sems[s], v)
                            lw[s] = v
                    if o["fn"] is not None:
                        instr = o["fn"](eng)
                        if o["inc"] is not None:
                            assert instr is not None
                            instr.then_inc(sems[o["inc"][0]], o["inc"][1])
                    else:
                        assert o["inc"] is None
            sec(body)


def build_nc(n_sub=NSUB, do_ffn=True, skip_readd=False):
    nc = bass.Bass("TRN2", target_bir_lowering=False)
    ap = {}
    ap["ximg"] = nc.dram_tensor("ximg", [P, DC * S], F32, kind="ExternalInput").ap()
    ap["wblob"] = nc.dram_tensor("wblob", [P, WBLOB_COLS], F32, kind="ExternalInput").ap()
    ap["bblob"] = nc.dram_tensor("bblob", [P, NSUB * 128], F32, kind="ExternalInput").ap()
    ap["identb"] = nc.dram_tensor("identb", [P, 288], F32, kind="ExternalInput").ap()
    if do_ffn:
        ap["fblob"] = nc.dram_tensor("fblob", [P, FBLOB_COLS], F32, kind="ExternalInput").ap()
        ap["fcb"] = nc.dram_tensor("fcb", [P, 64], F32, kind="ExternalInput").ap()
        ap["bdimg"] = nc.dram_tensor("bdimg", [SLOT, NEXP * D], F32, kind="ExternalInput").ap()
    ap["y"] = nc.dram_tensor("y", [P, DC * S], F32, kind="ExternalOutput").ap()

    pr = Prog()

    from contextlib import ExitStack
    with ExitStack() as _stk0:
        STREAM = _stk0.enter_context(nc.sbuf_tensor("stream", [P, DC * S], F32))
        XT = _stk0.enter_context(nc.sbuf_tensor("xt", [P, DC * T], BF16))
        BW = _stk0.enter_context(nc.sbuf_tensor("bw", [P, 1536], F32))
        HB = _stk0.enter_context(nc.sbuf_tensor("hb", [P, 1920], BF16))
        CONSTS = _stk0.enter_context(nc.sbuf_tensor("consts", [P, 64], F32))
        BIAS = _stk0.enter_context(nc.sbuf_tensor("biasb", [P, 128], F32))
        IDENTB = _stk0.enter_context(nc.sbuf_tensor("identb_sb", [P, 288], BF16))
        FCONST = _stk0.enter_context(nc.sbuf_tensor("fconst", [P, 64], F32))
        FB = _stk0.enter_context(nc.sbuf_tensor("fbias", [P, 64], F32))
        nc.const_aps.register(F32, float(EPS_TOP), BIAS[:, 104:105])
        nc.const_aps.register(F32, float(EPS_SUB), BIAS[:, 105:106])
        _sem_names = ["tensor", "vector", "scalar", "gpsimd", "sync",
                      "wdS", "wdA", "wdW1", "wdW2a", "wdW2b", "wdF", "wdE0",
                      "wdE1", "xdI", "xdB", "xdR", "xdFB", "xdBD", "out"]
        _sems = [_stk0.enter_context(nc.semaphore(f"sem_{n}")) for n in _sem_names]
        SEMS = dict(zip(_sem_names, _sems))

        def schunk(dc, lo, n):
            return STREAM[:, dc * S + lo: dc * S + lo + n]

        def xtc(dc):
            return XT[:, dc * T:(dc + 1) * T]

        def bwc(b, n=T):
            return BW[:, b:b + n]

        def hbc(b, n=T):
            return HB[:, b:b + n]

        def ccol(c):
            return CONSTS[:, c:c + 1]

        def bcolv(c):
            return BIAS[:, c:c + 1]

        ONES_COL = IDENTB[:, IB_ONECOL:IB_ONECOL + 1]
        ONES_ROW = IDENTB[0:1, IB_ONEROW:IB_ONEROW + 128]
        IDENT = IDENTB[:, 0:128]

        pr.dma("sync", "xdI", lambda e: e.dma_start(out=BW[:, 0:288], in_=ap["identb"]))
        pr.dma("sync", "xdI", lambda e: e.dma_start(out=BIAS[:], in_=ap["bblob"][:, 0:128]))
        if do_ffn:
            pr.dma("sync", "xdI", lambda e: e.dma_start(out=FCONST[:], in_=ap["fcb"]))
        w_init = pr.dma("sync", "xdI", lambda e: e.dma_start(out=STREAM[:], in_=ap["ximg"]))
        pr.op("vector", lambda e: e.tensor_copy(IDENTB[:], BW[:, 0:288]),
              waits=[w_init])
        w_ident = pr.fence("vector")
        w_bias0 = w_init
        w_fcb = w_init
        w_x = w_init

        ln_war = {"a": [], "b": []}

        def emit_stats(stat_bank, src_fn, wait_entry=()):
            f_mm = {}
            fv0 = pr.fence("vector")
            for dc in range(DC):
                src = src_fn(dc)
                ub = HB_UB0 if dc % 2 == 0 else HB_UB1
                if dc == 0:
                    ws = list(wait_entry) + [fv0] + ln_war["a"]
                elif dc == 1:
                    ws = list(ln_war["b"])
                else:
                    ws = [f_mm[dc - 2]]
                pr.op("scalar", lambda e, src=src, ub=ub: e.activation(
                    hbc(ub), src, AF.Identity), waits=ws)
                pr.op("scalar", lambda e, src=src, ub=ub: e.activation(
                    hbc(ub + 256), src, AF.Square))
                fs = pr.fence("scalar")
                pr.op("tensor", lambda e, b=stat_bank, ub=ub, f=(dc == 0),
                      l=(dc == DC - 1): e.matmul(
                          b[0:1, 0:512], ONES_COL, HB[:, ub:ub + 512],
                          start=f, stop=l), waits=[fs])
                f_mm[dc] = pr.fence("tensor")
            ln_war["a"] = [f_mm[DC - 2]]

        def emit_ln_finish(stat_bank, z_bank, eps, affine, src_fn, dst_fn,
                           double_eps2=None):
            ft = pr.fence("tensor")
            XA = stat_bank[0:1, 0:256]
            XB = stat_bank[0:1, 256:512]
            ZB = z_bank[0:1, 256:512]
            R = BW[0:1, BW4:BW4 + 256]

            pr.op("vector", lambda e: e.tensor_scalar(XA, XA, -1.0 / D, None, ALU.mult),
                  waits=[ft])
            pr.op("vector", lambda e: e.tensor_scalar(XB, XB, 1.0 / D, None, ALU.mult))
            fv = pr.fence("vector")
            pr.op("scalar", lambda e: e.activation(R, XA, AF.Square), waits=[fv])
            fs = pr.fence("scalar")
            pr.op("vector", lambda e: e.tensor_tensor(XB, XB, R, ALU.subtract),
                  waits=[fs])
            fv = pr.fence("vector")
            if double_eps2 is not None:
                pr.op("scalar", lambda e: e.activation(ZB, XB, AF.Identity), waits=[fv])
            pr.op("scalar", lambda e: e.activation(XB, XB, AF.Ln, bias=float(eps),
                                                   scale=1.0), waits=[fv])
            pr.op("scalar", lambda e: e.activation(XB, XB, AF.Exp, scale=-0.5))
            if double_eps2 is not None:
                pr.op("scalar", lambda e: e.activation(R, XB, AF.Square))
                fs = pr.fence("scalar")
                pr.op("vector", lambda e: e.tensor_tensor(ZB, ZB, R, ALU.mult),
                      waits=[fs])
                fv = pr.fence("vector")
                pr.op("scalar", lambda e: e.activation(ZB, ZB, AF.Ln,
                                                       bias=float(double_eps2),
                                                       scale=1.0), waits=[fv])
                pr.op("scalar", lambda e: e.activation(ZB, ZB, AF.Exp, scale=-0.5))
                pr.op("scalar", lambda e: e.activation(R, ZB, AF.Identity))
                fs = pr.fence("scalar")
                pr.op("vector", lambda e: e.tensor_tensor(XB, XB, R, ALU.mult),
                      waits=[fs])
            fv = pr.fence("vector")
            pr.op("scalar", lambda e: e.activation(R, XB, AF.Identity), waits=[fv])
            fs = pr.fence("scalar")
            pr.op("vector", lambda e: e.tensor_copy(HB[0:1, HB_RHI:HB_RHI + 256], R),
                  waits=[fs])
            ZA = z_bank[0:1, 0:256]
            pr.op("vector", lambda e: e.tensor_copy(ZA, HB[0:1, HB_RHI:HB_RHI + 256]))
            pr.op("vector", lambda e: e.tensor_tensor(ZA, R, ZA, ALU.subtract))
            pr.op("vector", lambda e: e.tensor_copy(HB[0:1, HB_RLO:HB_RLO + 256], ZA))
            pr.op("vector", lambda e: e.tensor_tensor(XA, XA, R, ALU.mult))
            pr.op("vector", lambda e: e.tensor_copy(HB[0:1, HB_BB:HB_BB + 256], XA))
            frow = pr.fence("vector")
            pr.op("tensor", lambda e, b=z_bank: e.matmul(
                b[:, 0:256], ONES_ROW, HB[0:1, HB_RHI:HB_RHI + 256],
                start=True, stop=False), waits=[frow])
            pr.op("tensor", lambda e, b=z_bank: e.matmul(
                b[:, 0:256], ONES_ROW, HB[0:1, HB_RLO:HB_RLO + 256],
                start=False, stop=True))
            pr.op("tensor", lambda e, b=z_bank: e.matmul(
                b[:, 256:512], ONES_ROW, HB[0:1, HB_BB:HB_BB + 256],
                start=False, stop=True))
            fb = pr.fence("tensor")
            ln_war["b"] = [fb]
            f_id = {}
            for dc in range(DC):
                src = src_fn(dc)
                dst = dst_fn(dc)
                slot = BW4 if dc % 2 == 0 else BW5
                wt_ap = [fb] if dc == 0 else []
                if dc >= 2:
                    wt_ap.append(f_id[dc - 2])
                pr.op("vector", lambda e, src=src, b=z_bank, sl=slot: e.tensor_tensor(
                    bwc(sl), src, b[:, 0:256], ALU.mult), waits=wt_ap)
                pr.op("vector", lambda e, b=z_bank, sl=slot: e.tensor_tensor(
                    bwc(sl), bwc(sl), b[:, 256:512], ALU.add))
                fv3 = pr.fence("vector")
                if affine is not None:
                    gb, bb2, ct = affine
                    pr.op("scalar", lambda e, dst=dst, gb=gb, bb2=bb2, ct=ct, dc=dc,
                          sl=slot: e.activation(dst, bwc(sl), AF.Identity,
                                                bias=ct[:, bb2 + dc:bb2 + dc + 1],
                                                scale=ct[:, gb + dc:gb + dc + 1]),
                          waits=[fv3])
                    f_id[dc] = pr.fence("scalar")
                else:
                    pr.op("scalar", lambda e, dst=dst, sl=slot: e.activation(
                        dst, bwc(sl), AF.Identity), waits=[fv3])
                    f_id[dc] = pr.fence("scalar")

        def cast_tile(t, waits=()):
            src = STREAM.ap().rearrange("p (c s) -> p c s", c=DC)[:, :, t * T:(t + 1) * T]
            dst = XT.ap().rearrange("p (c s) -> p c s", c=DC)
            pr.op("scalar", lambda e, src=src, dst=dst: e.activation(
                dst, src, AF.Identity), waits=waits)

        from contextlib import ExitStack
        with ExitStack() as _stk1:
            PB0, PB1, PB2, PB3, PB4, PB5, PB6, PB7 = [
                _stk1.enter_context(nc.psum_tensor(f"pb{j}", [P, 512], F32))
                for j in range(8)]
            WREG0 = _stk1.enter_context(nc.sbuf_tensor("wreg0", [P, 32768], BF16))
            WREG1 = _stk1.enter_context(nc.sbuf_tensor("wreg1", [P, 32768], BF16))
            PBX, PBZ = PB6, PB7

            for t in range(NT):
                emit_stats(PBX, lambda dc, t=t: schunk(dc, t * T, T),
                           wait_entry=[w_x] if t == 0 else ())
                emit_ln_finish(PBX, PBZ, EPS_TOP, None,
                               lambda dc, t=t: schunk(dc, t * T, T),
                               lambda dc, t=t: schunk(dc, t * T, T))

            WREGS = [WREG0, WREG1]

            def wk_t(R, dc, hc):
                return R[:, dc * 1024 + hc * 128: dc * 1024 + (hc + 1) * 128]

            def wv_t(R, dc, hc):
                return R[:, 8192 + dc * 1024 + hc * 128: 8192 + dc * 1024 + (hc + 1) * 128]

            def wq_t(R, dc, hc):
                return R[:, 16384 + dc * 1024 + hc * 128: 16384 + dc * 1024 + (hc + 1) * 128]

            def wo_t(R, hc, oc):
                return R[:, 24576 + hc * 1024 + oc * 128: 24576 + hc * 1024 + (oc + 1) * 128]

            def w1_t(R, dc, fc):
                return R[:, dc * 4096 + fc * 128: dc * 4096 + (fc + 1) * 128]

            def w2_t(R, fc, oc):
                return R[:, fc * 1024 + oc * 128: fc * 1024 + (oc + 1) * 128]

            reg_free = {(0, 0): [], (0, 1): [], (1, 0): [], (1, 1): []}
            bias_free = []

            for i in range(n_sub):
                Ra, Rb = WREGS[i % 2], WREGS[(i + 1) % 2]
                base = i * WSUB
                w_kv = pr.dma("gpsimd", "wdS", lambda e, b=base, R=Ra: e.dma_start(
                    out=R[:, 0:16384], in_=ap["wblob"][:, b:b + 16384]),
                    waits=reg_free[(i % 2, 0)])
                w_qo = pr.dma("gpsimd", "wdA", lambda e, b=base, R=Ra: e.dma_start(
                    out=R[:, 16384:32768], in_=ap["wblob"][:, b + 16384:b + 32768]),
                    waits=reg_free[(i % 2, 1)])
                pr.dma("gpsimd", "wdW1", lambda e, b=base, R=Rb: e.dma_start(
                    out=R[:, 0:16384], in_=ap["wblob"][:, b + 32768:b + 49152]),
                    waits=reg_free[((i + 1) % 2, 0)])
                w_w1 = pr.dma("gpsimd", "wdW1", lambda e, b=base, R=Rb: e.dma_start(
                    out=R[:, 16384:32768], in_=ap["wblob"][:, b + 49152:b + 65536]),
                    waits=reg_free[((i + 1) % 2, 1)])
                if i > 0:
                    w_bias = pr.dma("sync", "xdB", lambda e, i=i: e.dma_start(
                        out=BIAS[:], in_=ap["bblob"][:, i * 128:(i + 1) * 128]),
                        waits=bias_free)
                else:
                    w_bias = w_bias0

                f_er = {}
                f_pr = {}
                for t in range(NT):
                    cast_tile(t, waits=[pr.fence("tensor"), pr.fence("scalar")])
                    fcast = pr.fence("scalar")
                    for p in range(4):
                        g = t * 4 + p
                        pkb = PB0 if p % 2 == 0 else PB1
                        pvb = PB2 if p % 2 == 0 else PB3
                        first = (t == 0)
                        for h in (2 * p, 2 * p + 1):
                            hoff = (h % 2) * 256
                            wt = [fcast] + ([f_er[g - 2]] if g >= 2 else [])
                            for dc in range(DC):
                                pr.op("tensor", lambda e, pkb=pkb, hoff=hoff,
                                      R=Ra, dc=dc, h=h:
                                      e.matmul(pkb[:, hoff:hoff + 256],
                                               wk_t(R, dc, h), xtc(dc),
                                               start=(dc == 0), stop=(dc == DC - 1)),
                                      waits=(wt + ([w_kv] if (t == 0 and h == 0) else [])) if dc == 0 else [])
                            for dc in range(DC):
                                pr.op("tensor", lambda e, pvb=pvb, hoff=hoff,
                                      R=Ra, dc=dc, h=h:
                                      e.matmul(pvb[:, hoff:hoff + 256],
                                               wv_t(R, dc, h), xtc(dc),
                                               start=(dc == 0), stop=(dc == DC - 1)),
                                      waits=([f_pr[g - 2]] if (g >= 2 and dc == 0 and h % 2 == 0) else []))
                        fmm = pr.fence("tensor")
                        wsc = [fmm, w_bias] + ([f_pr[g - 1]] if g >= 1 else [])
                        for h in (2 * p, 2 * p + 1):
                            hoff = (h % 2) * 256
                            Ba = BW0 if h % 2 == 0 else BW1
                            Bb = BW2 if h % 2 == 0 else BW3
                            pr.op("scalar", lambda e, pkb=pkb, hoff=hoff, h=h,
                                  B=Ba: e.activation(
                                      bwc(B), pkb[:, hoff:hoff + 256], AF.Exp,
                                      bias=bcolv(B_BK + h), scale=1.0),
                                  waits=wsc if h % 2 == 0 else ())
                            pr.op("scalar", lambda e, pkb=pkb, hoff=hoff, h=h,
                                  B=Bb: e.activation(
                                      bwc(B), pkb[:, hoff:hoff + 256], AF.Relu,
                                      bias=bcolv(B_BK + h), scale=1.0,
                                      accum_out=ccol(C_T0 if h % 2 == 0 else C_T1)))
                        f_er[g] = pr.fence("scalar")
                        for h in (2 * p, 2 * p + 1):
                            Ba = BW0 if h % 2 == 0 else BW1
                            pr.op("vector", lambda e, h=h, B=Ba, first=first:
                                  e.tensor_scalar(bwc(B), bwc(B), 1.0,
                                                  None if first else ccol(C_SUME + h),
                                                  ALU.min, ALU.add,
                                                  accum_out=ccol(C_SUME + h)),
                                  waits=[f_er[g]] if h % 2 == 0 else ())
                            ct = C_T0 if h % 2 == 0 else C_T1
                            pr.op("vector", lambda e, h=h, ct=ct, first=first:
                                  e.tensor_scalar(ccol(C_SUMR + h), ccol(ct),
                                                  0.0 if first else ccol(C_SUMR + h),
                                                  None, ALU.add))
                        pr.op("vector", lambda e: e.tensor_tensor(
                            BW[:, BW0:BW0 + 512], BW[:, BW0:BW0 + 512],
                            BW[:, BW2:BW2 + 512], ALU.add))
                        pr.op("vector", lambda e, pvb=pvb: e.tensor_tensor(
                            BW[:, BW2:BW2 + 512], BW[:, BW0:BW0 + 512],
                            pvb[:, 0:512], ALU.mult))
                        for h in (2 * p, 2 * p + 1):
                            Bb = BW2 if h % 2 == 0 else BW3
                            ct = C_T0 if h % 2 == 0 else C_T1
                            pr.op("vector", lambda e, B2=Bb, ct=ct: e.reduce_sum(
                                ccol(ct), bwc(B2), axis=AX.X))
                            pr.op("vector", lambda e, h=h, ct=ct, first=first:
                                  e.tensor_scalar(ccol(C_SUMKV + h), ccol(ct),
                                                  0.0 if first else ccol(C_SUMKV + h),
                                                  None, ALU.add))
                        f_pr[g] = pr.fence("vector")
                for hc in range(8):
                    pr.op("vector", lambda e, hc=hc: e.tensor_scalar(
                        ccol(C_SUMK + hc), ccol(C_SUME + hc),
                        ccol(C_SUMR + hc), None, ALU.add),
                        waits=[w_bias] if hc == 0 else ())
                    pr.op("vector", lambda e, hc=hc: e.tensor_scalar(
                        ccol(C_T0), ccol(C_SUMK + hc), bcolv(B_BV + hc),
                        None, ALU.mult))
                    pr.op("vector", lambda e, hc=hc: e.tensor_scalar(
                        ccol(C_KV + hc), ccol(C_T0), ccol(C_SUMKV + hc),
                        None, ALU.add))
                f_after_S = pr.fence("tensor")
                f_S_v = pr.fence("vector")
                w_w2a = pr.dma("gpsimd", "wdW2a", lambda e, b=base, R=Ra: e.dma_start(
                    out=R[:, 0:16384], in_=ap["wblob"][:, b + 65536:b + 81920]),
                    waits=[f_after_S])

                def A_D1(t):
                    emit_stats(PBX, lambda dc, t=t: schunk(dc, t * T, T))

                def A_D3(t):
                    emit_ln_finish(PBX, PBZ, EPS_SUB, (B_L1S, B_L1B, BIAS),
                                   lambda dc, t=t: schunk(dc, t * T, T),
                                   lambda dc, t=t: schunk(dc, t * T, T))

                f_aev = {}
                f_att = {}
                f_wo = {}
                f_evS = pr.fence("scalar")
                for t in range(NT):
                    cast_tile(t, waits=[pr.fence("tensor"), pr.fence("scalar")])
                    fcast = pr.fence("scalar")
                    for p in range(4):
                        g = t * 4 + p
                        pqb = PB0 if p % 2 == 0 else PB1
                        for h in (2 * p, 2 * p + 1):
                            hoff = (h % 2) * 256
                            wt = [fcast] + ([f_aev[g - 2]] if g >= 2 else
                                            [pr.fence("scalar")])
                            for dc in range(DC):
                                pr.op("tensor", lambda e, pqb=pqb, hoff=hoff,
                                      R=Ra, dc=dc, h=h:
                                      e.matmul(pqb[:, hoff:hoff + 256],
                                               wq_t(R, dc, h),
                                               xtc(dc), start=(dc == 0),
                                               stop=(dc == DC - 1)),
                                      waits=(wt + ([w_qo] if (t == 0 and h == 0) else [])) if dc == 0 else [])
                        fmm = pr.fence("tensor")
                        wsc = [fmm] + ([f_att[g - 1]] if g >= 1 else [f_S_v])
                        for h in (2 * p, 2 * p + 1):
                            hoff = (h % 2) * 256
                            Ba = BW0 if h % 2 == 0 else BW1
                            pr.op("scalar", lambda e, pqb=pqb, hoff=hoff, h=h,
                                  B=Ba: e.activation(
                                      bwc(B), pqb[:, hoff:hoff + 256],
                                      AF.Identity, bias=bcolv(B_BQ + h),
                                      scale=1.0),
                                  waits=wsc if h % 2 == 0 else ())
                        f_aev[g] = pr.fence("scalar")
                        pr.op("scalar", lambda e: e.activation(
                            BW[:, BW2:BW2 + 512], BW[:, BW0:BW0 + 512], AF.Exp))
                        fs_e = pr.fence("scalar")
                        pr.op("vector", lambda e: e.tensor_scalar(
                            BW[:, BW2:BW2 + 512], BW[:, BW2:BW2 + 512], 1.0,
                            None, ALU.min), waits=[fs_e])
                        pr.op("vector", lambda e: e.tensor_scalar(
                            BW[:, BW0:BW0 + 512], BW[:, BW0:BW0 + 512], 0.0,
                            None, ALU.max))
                        pr.op("vector", lambda e: e.tensor_tensor(
                            BW[:, BW0:BW0 + 512], BW[:, BW2:BW2 + 512],
                            BW[:, BW0:BW0 + 512], ALU.add))
                        for h in (2 * p, 2 * p + 1):
                            Ba = BW0 if h % 2 == 0 else BW1
                            Bb = BW2 if h % 2 == 0 else BW3
                            pr.op("vector", lambda e, h=h, B=Ba, B2=Bb:
                                  e.tensor_scalar(bwc(B2), bwc(B),
                                                  ccol(C_SUMK + h), 1e-6,
                                                  ALU.mult, ALU.add))
                            pr.op("vector", lambda e, h=h, B=Ba: e.tensor_scalar(
                                bwc(B), bwc(B), ccol(C_KV + h), None, ALU.mult))
                        fv_c = pr.fence("vector")
                        pr.op("scalar", lambda e: e.activation(
                            BW[:, BW2:BW2 + 512], BW[:, BW2:BW2 + 512], AF.Ln),
                            waits=[fv_c])
                        pr.op("scalar", lambda e: e.activation(
                            BW[:, BW2:BW2 + 512], BW[:, BW2:BW2 + 512], AF.Exp,
                            scale=-1.0))
                        fs_z = pr.fence("scalar")
                        wv2 = [fs_z] + ([f_wo[g - 1]] if g >= 1 else [])
                        pr.op("vector", lambda e: e.tensor_tensor(
                            HB[:, 0:512], BW[:, BW0:BW0 + 512],
                            BW[:, BW2:BW2 + 512], ALU.mult), waits=wv2)
                        f_att[g] = pr.fence("vector")
                        for h in (2 * p, 2 * p + 1):
                            slot = HB_ROT0 if h % 2 == 0 else HB_ROT1
                            for oc in range(8):
                                pu = [PB2, PB3, PB4, PB5][oc // 2]
                                colr = slice((oc % 2) * 256, (oc % 2) * 256 + 256)
                                wt2 = []
                                if oc == 0:
                                    wt2 = [f_att[g]]
                                    if h == 0:
                                        wt2.append(f_evS)
                                pr.op("tensor", lambda e, pu=pu, colr=colr, R=Ra,
                                      h=h, oc=oc, slot=slot,
                                      st=(h == 0 and oc % 2 == 0):
                                      e.matmul(pu[:, colr], wo_t(R, h, oc),
                                               hbc(slot), start=st, stop=(h == 7)),
                                      waits=wt2)
                        f_wo[g] = pr.fence("tensor")
                        if t > 0:
                            if p == 0:
                                A_D1(t - 1)
                            elif p == 2:
                                A_D3(t - 1)
                    fmm = pr.fence("tensor")
                    fv_chain = pr.fence("vector")
                    for oc in range(8):
                        pu = [PB2, PB3, PB4, PB5][oc // 2]
                        colr = slice((oc % 2) * 256, (oc % 2) * 256 + 256)
                        bslot = BW4 if oc % 2 == 0 else BW5
                        wv3 = [fmm, fv_chain] if oc == 0 else []
                        if oc >= 2:
                            wv3 = [pr.fence("vector")]
                        pr.op("scalar", lambda e, pu=pu, colr=colr, oc=oc, b=bslot:
                              e.activation(bwc(b), pu[:, colr], AF.Gelu_apprx_tanh,
                                           bias=bcolv(B_BO + oc), scale=1.0),
                              waits=wv3)
                        fs = pr.fence("scalar")
                        pr.op("vector", lambda e, oc=oc, t=t, b=bslot:
                              e.tensor_tensor(schunk(oc, t * T, T),
                                              schunk(oc, t * T, T), bwc(b),
                                              ALU.add), waits=[fs])
                    f_evS = pr.fence("scalar")
                A_D1(NT - 1)
                A_D3(NT - 1)

                f_after_A = pr.fence("tensor")
                w_w2b = pr.dma("gpsimd", "wdW2b", lambda e, b=base, R=Ra: e.dma_start(
                    out=R[:, 16384:32768], in_=ap["wblob"][:, b + 81920:b + 98304]),
                    waits=[f_after_A])

                def M_D1(t):
                    emit_stats(PBX, lambda dc, t=t: schunk(dc, t * T, T))

                def M_D3(t):
                    emit_ln_finish(PBX, PBZ, EPS_SUB, (B_L2S, B_L2B, BIAS),
                                   lambda dc, t=t: schunk(dc, t * T, T),
                                   lambda dc, t=t: schunk(dc, t * T, T))

                f_h1 = {}
                f_out = {}
                f_evS2 = pr.fence("scalar")
                for t in range(NT):
                    cast_tile(t, waits=[pr.fence("tensor"), pr.fence("scalar")])
                    fcast = pr.fence("scalar")
                    for fc in range(FC + 1):
                        if fc < FC:
                            ph = PB4 if fc % 2 == 0 else PB5
                            k2 = t * FC + fc - 2
                            wt = [fcast] if fc == 0 else []
                            if k2 >= 0:
                                wt.append(f_h1[k2])
                            elif t == 0 and fc < 2:
                                wt.append(pr.fence("scalar"))
                            for dc in range(DC):
                                pr.op("tensor", lambda e, ph=ph, R=Rb, dc=dc, fc=fc:
                                      e.matmul(ph[:, 0:256], w1_t(R, dc, fc),
                                               xtc(dc), start=(dc == 0),
                                               stop=(dc == DC - 1)),
                                      waits=(wt + ([w_w1] if (t == 0 and fc == 0) else [])) if dc == 0 else [])
                            fmm = pr.fence("tensor")
                            slot = HB_ROT0 if fc % 2 == 0 else HB_ROT1
                            k2o = t * FC + fc - 2
                            wsc = [fmm]
                            if k2o >= 0:
                                wsc.append(f_out[k2o])
                            pr.op("scalar", lambda e, ph=ph, fc=fc, slot=slot:
                                  e.activation(hbc(slot), ph[:, 0:256],
                                               AF.Gelu_apprx_tanh,
                                               bias=bcolv(B_B1 + fc), scale=1.0),
                                  waits=wsc)
                            f_h1[t * FC + fc] = pr.fence("scalar")
                        if fc >= 1:
                            f2 = fc - 1
                            slot = HB_ROT0 if f2 % 2 == 0 else HB_ROT1
                            for oc in range(8):
                                po = [PB0, PB1, PB2, PB3][oc // 2]
                                colr = slice((oc % 2) * 256, (oc % 2) * 256 + 256)
                                wt2 = []
                                if oc == 0:
                                    wt2 = [f_h1[t * FC + f2]]
                                    if f2 == 0:
                                        wt2.append(f_evS2)
                                pr.op("tensor", lambda e, po=po, colr=colr, R=Ra,
                                      f2=f2, oc=oc, slot=slot,
                                      st=(f2 == 0 and oc % 2 == 0):
                                      e.matmul(po[:, colr], w2_t(R, f2, oc),
                                               hbc(slot), start=st,
                                               stop=(f2 == FC - 1)),
                                      waits=wt2 + ([w_w2a] if (t == 0 and f2 == 0 and oc == 0) else []) +
                                            ([w_w2b] if (t == 0 and f2 == 16 and oc == 0) else []))
                            f_out[t * FC + f2] = pr.fence("tensor")
                        if t > 0:
                            if fc == 2:
                                M_D1(t - 1)
                            elif fc == 12:
                                M_D3(t - 1)
                    fmm = pr.fence("tensor")
                    fv_cast = pr.fence("vector")
                    fs_evac = pr.fence("scalar")
                    for oc in range(8):
                        po = [PB0, PB1, PB2, PB3][oc // 2]
                        colr = slice((oc % 2) * 256, (oc % 2) * 256 + 256)
                        bslot = BW0 if oc % 2 == 0 else BW3
                        wv3 = [fmm, fv_cast] if oc == 0 else []
                        if oc >= 2:
                            wv3 = [pr.fence("vector")]
                        pr.op("scalar", lambda e, po=po, colr=colr, oc=oc, b=bslot:
                              e.activation(bwc(b), po[:, colr], AF.Gelu_apprx_tanh,
                                           bias=bcolv(B_B2 + oc), scale=1.0),
                              waits=wv3)
                        fs = pr.fence("scalar")
                        pr.op("vector", lambda e, oc=oc, t=t, b=bslot:
                              e.tensor_tensor(schunk(oc, t * T, T),
                                              schunk(oc, t * T, T), bwc(b),
                                              ALU.add), waits=[fs])
                    f_evS2 = pr.fence("scalar")
                M_D1(NT - 1)
                M_D3(NT - 1)

                f_after_M = pr.fence("tensor")
                fs_end = pr.fence("scalar")
                fv_end = pr.fence("vector")
                for k in reg_free:
                    reg_free[k] = [f_after_M]
                bias_free = [fs_end, fv_end]

            if not skip_readd:
                for dc in range(DC):
                    for half in range(2):
                        lo = half * 1024
                        fv = pr.fence("vector")
                        fs = pr.fence("scalar")
                        wdx = pr.dma("sync", "xdR", lambda e, dc=dc, lo=lo:
                                     e.dma_start(out=BW[:, 0:1024],
                                                 in_=ap["ximg"][:, dc * S + lo: dc * S + lo + 1024]),
                                     waits=[fv, fs])
                        pr.op("vector", lambda e, dc=dc, lo=lo: e.tensor_tensor(
                            schunk(dc, lo, 1024), schunk(dc, lo, 1024),
                            BW[:, 0:1024], ALU.add), waits=[wdx])

        pr.lint()
        with nc.Block() as blk:
            pr.replay(blk, SEMS)

        if do_ffn:
            f_attn_t = pr.fence("tensor")
            from contextlib import ExitStack
            with ExitStack() as _stk2:
                PF0, PF1, PF2, PF3, PF4, PF5 = [
                    _stk2.enter_context(nc.psum_tensor(f"pf{j}", [P, 512], F32))
                    for j in range(6)]
                PT0 = _stk2.enter_context(nc.psum_tensor("pt0", [P, 1024], BF16))
                PT1 = _stk2.enter_context(nc.psum_tensor("pt1", [P, 1024], BF16))
                pr.op("vector", lambda e: e.memset(BW[0:1, BW4:BW4 + 128], 1.0),
                      waits=[f_attn_t])
                fvb = pr.fence("vector")
                pr.op("tensor", lambda e: e.matmul(
                    PF0[:, 0:1], BW[0:1, BW4:BW4 + 128],
                    FCONST[0:1, FC_SCALE:FC_SCALE + 1], start=True, stop=True),
                    waits=[fvb, w_fcb])
                fmm0 = pr.fence("tensor")
                pr.op("scalar", lambda e: e.activation(
                    ccol(C_SCL), PF0[:, 0:1], AF.Identity), waits=[fmm0])
                fg = pr.fence("scalar")
                for dc in range(DC):
                    pr.op("vector", lambda e, dc=dc: e.tensor_scalar(
                        ccol(C_SG + dc), FCONST[:, FC_SG + dc:FC_SG + dc + 1],
                        ccol(C_SCL), None, ALU.mult),
                        waits=[fg] if dc == 0 else ())
                    pr.op("vector", lambda e, dc=dc: e.tensor_scalar(
                        ccol(C_SB + dc), FCONST[:, FC_SB + dc:FC_SB + dc + 1],
                        ccol(C_SCL), None, ALU.mult))

                def emit_xn(XND):
                    for t in range(NT):
                        sb = PF2 if t % 2 == 0 else PF3
                        emit_stats(sb, lambda dc, t=t: schunk(dc, t * T, T))
                        emit_ln_finish(sb, PF4, EPS_TOP, (FC_NG, FC_NB, FCONST),
                                       lambda dc, t=t: schunk(dc, t * T, T),
                                       lambda dc, t=t, X=XND:
                                       X[:, dc * S + t * T: dc * S + t * T + T],
                                       double_eps2=EPS_FFN)

                def emit_se(SED, w_se):
                    for sl in range(4):
                        sb = PF2 if sl % 2 == 0 else PF3
                        emit_stats(sb,
                                   lambda dc, sl=sl, X=SED:
                                   X[:, dc * 1024 + sl * 256: dc * 1024 + sl * 256 + 256],
                                   wait_entry=[w_se] if sl == 0 else ())
                        emit_ln_finish(sb, PF4, EPS_FFN, (C_SG, C_SB, CONSTS),
                                       lambda dc, sl=sl, X=SED:
                                       X[:, dc * 1024 + sl * 256: dc * 1024 + sl * 256 + 256],
                                       lambda dc, sl=sl, X=SED:
                                       X[:, dc * 1024 + sl * 256: dc * 1024 + sl * 256 + 256])

                with nc.sbuf_tensor("slots", [P, 8192], BF16) as SLOTS:
                    with (
                        nc.sbuf_tensor("xn", [P, DC * S], BF16) as XN,
                        nc.sbuf_tensor("se", [P, 8192], BF16) as SE,
                        nc.sbuf_tensor("dspt", [P, 16384], BF16) as DSPT,
                        nc.sbuf_tensor("xnt", [P, 16384], BF16) as XNT,
                    ):
                        emit_xn(XN)
                        w_se = pr.dma("gpsimd", "wdF", lambda e: e.dma_start(
                            out=SE[:], in_=ap["fblob"][:, 0:8192]),
                            waits=[f_attn_t])
                        emit_se(SE, w_se)
                        f_xn_v = pr.fence("vector")
                        f_xn_s = pr.fence("scalar")
                        PFS = [PF0, PF1, PF2, PF3]
                        accs = [C_F1, C_F2, C_F3, C_F4]
                        for ec in range(8):
                            wt = ([f_xn_v, f_xn_s] if ec == 0 else
                                  [pr.fence("vector"), pr.fence("scalar")])
                            for sl4 in range(4):
                                for dc in range(DC):
                                    pr.op("tensor", lambda e, p=PFS[sl4], dc=dc,
                                          ec=ec, sl4=sl4: e.matmul(
                                              p[:, 0:512],
                                              SE[:, dc * 1024 + ec * 128: dc * 1024 + (ec + 1) * 128],
                                              XN[:, dc * S + sl4 * 512: dc * S + sl4 * 512 + 512],
                                              start=(dc == 0), stop=(dc == DC - 1)),
                                          waits=(wt if (dc == 0 and sl4 == 0) else []))
                            fmm = pr.fence("tensor")
                            pr.op("vector", lambda e: e.reduce_max(
                                ccol(C_F0), PF0[:, 0:512], axis=AX.X, negate=True),
                                waits=[fmm])
                            for sl4 in range(1, 4):
                                pr.op("vector", lambda e, p=PFS[sl4]: e.reduce_max(
                                    ccol(C_F1), p[:, 0:512], axis=AX.X, negate=True))
                                pr.op("vector", lambda e: e.tensor_scalar(
                                    ccol(C_F0), ccol(C_F1), ccol(C_F0), None,
                                    ALU.min))
                            fv = pr.fence("vector")
                            for sl4 in range(4):
                                pr.op("scalar", lambda e, p=PFS[sl4], sl4=sl4:
                                      e.activation(XT[:, sl4 * 512:(sl4 + 1) * 512],
                                                   p[:, 0:512], AF.Exp,
                                                   bias=ccol(C_F0), scale=1.0,
                                                   accum_out=ccol(accs[sl4])),
                                      waits=[fv] if sl4 == 0 else ())
                            fs = pr.fence("scalar")
                            pr.op("vector", lambda e: e.tensor_scalar(
                                ccol(C_F1), ccol(C_F2), ccol(C_F1), None, ALU.add),
                                waits=[fs])
                            pr.op("vector", lambda e: e.tensor_scalar(
                                ccol(C_F1), ccol(C_F3), ccol(C_F1), None, ALU.add))
                            pr.op("vector", lambda e: e.tensor_scalar(
                                ccol(C_F1), ccol(C_F4), ccol(C_F1), None, ALU.add))
                            pr.op("vector", lambda e: e.reciprocal(
                                ccol(C_F2), ccol(C_F1)))
                            pr.op("vector", lambda e: e.tensor_scalar(
                                XT[:], XT[:], ccol(C_F2), None, ALU.mult))
                            fv = pr.fence("vector")
                            f_tr = {}
                            for tc in range(16):
                                pt = PT0 if tc % 2 == 0 else PT1
                                wt3 = [fv] if tc == 0 else []
                                if tc >= 2:
                                    wt3.append(f_tr[tc - 2])
                                pr.op("tensor", lambda e, pt=pt, tc=tc: e.transpose(
                                    pt[:, 0:128], XT[:, tc * 128:(tc + 1) * 128],
                                    IDENT), waits=wt3)
                                fmm2 = pr.fence("tensor")
                                pr.op("vector", lambda e, pt=pt, tc=tc, ec=ec:
                                      e.tensor_copy(
                                          DSPT[:, tc * 1024 + ec * 128: tc * 1024 + (ec + 1) * 128],
                                          pt[:, 0:128]), waits=[fmm2])
                                f_tr[tc] = pr.fence("vector")
                        fv = pr.fence("vector")
                        f_tr = {}
                        for blk_i in range(128):
                            dc, tc = blk_i // 16, blk_i % 16
                            pt = PT0 if blk_i % 2 == 0 else PT1
                            wt3 = [fv] if blk_i == 0 else []
                            if blk_i >= 2:
                                wt3.append(f_tr[blk_i - 2])
                            pr.op("tensor", lambda e, pt=pt, dc=dc, tc=tc:
                                  e.transpose(pt[:, 0:128],
                                              XN[:, dc * S + tc * 128: dc * S + (tc + 1) * 128],
                                              IDENT), waits=wt3)
                            fmm2 = pr.fence("tensor")
                            pr.op("vector", lambda e, pt=pt, dc=dc, tc=tc:
                                  e.tensor_copy(
                                      XNT[:, tc * 1024 + dc * 128: tc * 1024 + (dc + 1) * 128],
                                      pt[:, 0:128]), waits=[fmm2])
                            f_tr[blk_i] = pr.fence("vector")
                        fv = pr.fence("vector")
                        f_sl = {}
                        for j in range(16):
                            dc, eh = j // 2, j % 2
                            pf = PF0 if j % 2 == 0 else PF1
                            wt4 = [fv] if j == 0 else []
                            if j >= 2:
                                wt4.append(f_sl[j - 2])
                            for tc in range(16):
                                pr.op("tensor", lambda e, pf=pf, dc=dc, eh=eh,
                                      tc=tc: e.matmul(
                                          pf[:, 0:512],
                                          XNT[:, tc * 1024 + dc * 128: tc * 1024 + (dc + 1) * 128],
                                          DSPT[:, tc * 1024 + eh * 512: tc * 1024 + eh * 512 + 512],
                                          start=(tc == 0), stop=(tc == 15)),
                                      waits=(wt4 if tc == 0 else []))
                            fmm2 = pr.fence("tensor")
                            pr.op("vector", lambda e, pf=pf, dc=dc, eh=eh:
                                  e.tensor_copy(
                                      SLOTS[:, dc * 1024 + eh * 512: dc * 1024 + eh * 512 + 512],
                                      pf[:, 0:512]), waits=[fmm2])
                            f_sl[j] = pr.fence("vector")
                    f_slots_t = pr.fence("tensor")
                    f_slots_v = pr.fence("vector")

                    _stkE = ExitStack()
                    EOUT = _stkE.enter_context(nc.sbuf_tensor("eout", [P, 8192], BF16))
                    with (
                        nc.sbuf_tensor("eh0", [P, 24576], BF16) as EH0,
                        nc.sbuf_tensor("eh1", [P, 24576], BF16) as EH1,
                    ):
                        EHS = [EH0, EH1]
                        eh_free = [[f_slots_t], [f_slots_t]]
                        fb_free = []
                        bd_free = [f_slots_v, pr.fence("scalar")]
                        f_act = {}
                        f_gu = {}
                        f_wd = {}
                        for ex in range(NEXP):
                            ebase = 8192 + ex * EXP_COLS
                            w_fb = pr.dma("sync", "xdFB", lambda e, eb=ebase:
                                          e.dma_start(out=FB[:],
                                                      in_=ap["fblob"][:, eb + 49152:eb + 49216]),
                                          waits=fb_free)
                            w_bd = pr.dma("sync", "xdBD", lambda e, ex=ex:
                                          e.dma_start(out=BW[0:SLOT, 0:1024],
                                                      in_=ap["bdimg"][:, ex * D:(ex + 1) * D]),
                                          waits=bd_free)
                            w_eh = [None, None]
                            for h in range(2):
                                si = (2 * ex + h) % 2
                                w_eh[h] = pr.dma("gpsimd", "wdE0" if si == 0 else "wdE1",
                                                 lambda e, eb=ebase, h=h, si=si:
                                                 e.dma_start(out=EHS[si][:],
                                                             in_=ap["fblob"][:, eb + h * 24576: eb + (h + 1) * 24576]),
                                                 waits=eh_free[si])
                            for h in range(2):
                                EH = EHS[(2 * ex + h) % 2]
                                for fq in range(8):
                                    g = 16 * ex + 8 * h + fq
                                    pg = PF2 if fq % 2 == 0 else PF3
                                    pe = PF4 if fq % 2 == 0 else PF5
                                    wt5 = [w_eh[h]] if fq == 0 else []
                                    if g >= 2:
                                        wt5.append(f_act[g - 2])
                                    if ex == 0 and h == 0 and fq == 0:
                                        wt5.append(f_slots_v)
                                    for dc in range(DC):
                                        pr.op("tensor", lambda e, pg=pg, EH=EH,
                                              dc=dc, fq=fq, ex=ex: e.matmul(
                                                  pg[:, 0:64],
                                                  EH[:, dc * 1024 + fq * 128: dc * 1024 + (fq + 1) * 128],
                                                  SLOTS[:, dc * 1024 + ex * 64: dc * 1024 + ex * 64 + 64],
                                                  start=(dc == 0),
                                                  stop=(dc == DC - 1)),
                                              waits=(wt5 if dc == 0 else []))
                                    for dc in range(DC):
                                        pr.op("tensor", lambda e, pg=pg, EH=EH,
                                              dc=dc, fq=fq, ex=ex: e.matmul(
                                                  pg[:, 64:128],
                                                  EH[:, 8192 + dc * 1024 + fq * 128: 8192 + dc * 1024 + (fq + 1) * 128],
                                                  SLOTS[:, dc * 1024 + ex * 64: dc * 1024 + ex * 64 + 64],
                                                  start=(dc == 0),
                                                  stop=(dc == DC - 1)))
                                    fmm3 = pr.fence("tensor")
                                    wsc = [fmm3]
                                    if h == 0 and fq == 0:
                                        wsc.append(w_fb)
                                    if g >= 2:
                                        wsc.append(f_gu[g - 2])
                                    pr.op("scalar", lambda e, pg=pg, pe=pe, h=h,
                                          fq=fq: e.activation(
                                              pe[:, 0:64], pg[:, 0:64], AF.Gelu,
                                              bias=FB[:, h * 8 + fq:h * 8 + fq + 1],
                                              scale=1.0), waits=wsc)
                                    ups = BW4 if fq % 2 == 0 else BW4 + 64
                                    pr.op("scalar", lambda e, pg=pg, h=h, fq=fq,
                                          u=ups: e.activation(
                                              BW[:, u:u + 64], pg[:, 64:128],
                                              AF.Identity,
                                              bias=FB[:, 16 + h * 8 + fq:16 + h * 8 + fq + 1],
                                              scale=1.0))
                                    f_act[g] = fs2 = pr.fence("scalar")
                                    gus = HB_GU0 if fq % 2 == 0 else HB_GU1
                                    wv5 = [fs2]
                                    if g >= 2:
                                        wv5.append(f_wd[g - 2])
                                    pr.op("vector", lambda e, pe=pe, gus=gus,
                                          u=ups: e.tensor_tensor(
                                              HB[:, gus:gus + 64], pe[:, 0:64],
                                              BW[:, u:u + 64], ALU.mult),
                                          waits=wv5)
                                    fv5 = pr.fence("vector")
                                    f_gu[g] = fv5
                                    prow = slice((ex % 2) * SLOT, (ex % 2) * SLOT + SLOT)
                                    for dh in range(2):
                                        pw = PF0 if dh == 0 else PF1
                                        wt6 = [fv5] if dh == 0 else []
                                        if h == 0 and fq == 0 and dh == 0 and ex > 0:
                                            wt6.append(pr.fence("vector"))
                                        pr.op("tensor", lambda e, pw=pw, EH=EH,
                                              fq=fq, dh=dh, gus=gus, prow=prow,
                                              st=(h == 0 and fq == 0):
                                              e.matmul(pw[prow, 0:512],
                                                       HB[:, gus:gus + 64],
                                                       EH[:, 16384 + fq * 1024 + dh * 512: 16384 + fq * 1024 + dh * 512 + 512],
                                                       start=st,
                                                       stop=(h == 1 and fq == 7)),
                                              waits=wt6)
                                    f_wd[g] = pr.fence("tensor")
                                eh_free[(2 * ex + h) % 2] = [pr.fence("tensor")]
                            fmm4 = pr.fence("tensor")
                            prow = slice((ex % 2) * SLOT, (ex % 2) * SLOT + SLOT)
                            for dh in range(2):
                                pw = PF0 if dh == 0 else PF1
                                pr.op("vector", lambda e, pw=pw, dh=dh, ex=ex,
                                      prow=prow: e.tensor_tensor(
                                          EOUT[prow, (ex // 2) * 1024 + dh * 512: (ex // 2) * 1024 + dh * 512 + 512],
                                          pw[prow, 0:512],
                                          BW[0:SLOT, dh * 512:dh * 512 + 512],
                                          ALU.add),
                                      waits=[fmm4, w_bd] if dh == 0 else [])
                            fb_free = [pr.fence("scalar")]
                            bd_free = [pr.fence("vector")]

                    with (
                        nc.sbuf_tensor("xn2", [P, DC * S], BF16) as XN2,
                        nc.sbuf_tensor("se2", [P, 8192], BF16) as SE2,
                    ):
                        emit_xn(XN2)
                        w_se2 = pr.dma("gpsimd", "wdF", lambda e: e.dma_start(
                            out=SE2[:], in_=ap["fblob"][:, 0:8192]),
                            waits=[pr.fence("tensor")])
                        emit_se(SE2, w_se2)
                        f_pre_v = pr.fence("vector")
                        f_pre_s = pr.fence("scalar")
                        for tc in range(16):
                            wt7 = ([f_pre_v, f_pre_s] if tc == 0 else
                                   [pr.fence("vector"), pr.fence("scalar")])
                            for dc in range(DC):
                                pr.op("tensor", lambda e, dc=dc, tc=tc: e.matmul(
                                    PF2[:, 0:512],
                                    XN2[:, dc * S + tc * 128: dc * S + (tc + 1) * 128],
                                    SE2[:, dc * 1024: dc * 1024 + 512],
                                    start=(dc == 0), stop=(dc == DC - 1)),
                                    waits=(wt7 if dc == 0 else []))
                                pr.op("tensor", lambda e, dc=dc, tc=tc: e.matmul(
                                    PF3[:, 0:512],
                                    XN2[:, dc * S + tc * 128: dc * S + (tc + 1) * 128],
                                    SE2[:, dc * 1024 + 512: dc * 1024 + 1024],
                                    start=(dc == 0), stop=(dc == DC - 1)))
                            fmm5 = pr.fence("tensor")
                            pr.op("vector", lambda e: e.reduce_max(
                                ccol(C_F0), PF2[:, 0:512], axis=AX.X, negate=True),
                                waits=[fmm5])
                            pr.op("vector", lambda e: e.reduce_max(
                                ccol(C_F1), PF3[:, 0:512], axis=AX.X, negate=True))
                            pr.op("vector", lambda e: e.tensor_scalar(
                                ccol(C_F0), ccol(C_F1), ccol(C_F0), None, ALU.min))
                            fv6 = pr.fence("vector")
                            pr.op("scalar", lambda e: e.activation(
                                XT[:, 0:512], PF2[:, 0:512], AF.Exp,
                                bias=ccol(C_F0), scale=1.0,
                                accum_out=ccol(C_F1)), waits=[fv6])
                            pr.op("scalar", lambda e: e.activation(
                                XT[:, 512:1024], PF3[:, 0:512], AF.Exp,
                                bias=ccol(C_F0), scale=1.0,
                                accum_out=ccol(C_F2)))
                            fs3 = pr.fence("scalar")
                            pr.op("vector", lambda e: e.tensor_scalar(
                                ccol(C_F1), ccol(C_F2), ccol(C_F1), None, ALU.add),
                                waits=[fs3])
                            pr.op("vector", lambda e: e.reciprocal(
                                ccol(C_F2), ccol(C_F1)))
                            pr.op("vector", lambda e: e.tensor_scalar(
                                XT[:, 0:1024], XT[:, 0:1024], ccol(C_F2), None,
                                ALU.mult))
                            fv6 = pr.fence("vector")
                            f_tr = {}
                            for esc in range(8):
                                pt = PT0 if esc % 2 == 0 else PT1
                                wt8 = [fv6] if esc == 0 else []
                                if esc >= 2:
                                    wt8.append(f_tr[esc - 2])
                                pr.op("tensor", lambda e, pt=pt, esc=esc:
                                      e.transpose(pt[:, 0:128],
                                                  XT[:, esc * 128:(esc + 1) * 128],
                                                  IDENT), waits=wt8)
                                fmm6 = pr.fence("tensor")
                                pr.op("vector", lambda e, pt=pt, esc=esc:
                                      e.tensor_copy(
                                          XT[:, 1024 + esc * 128: 1024 + (esc + 1) * 128],
                                          pt[:, 0:128]), waits=[fmm6])
                                f_tr[esc] = pr.fence("vector")
                            fv7 = pr.fence("vector")
                            for oc in range(8):
                                pw = PF4 if oc < 4 else PF5
                                colr = slice((oc % 4) * 128, (oc % 4) * 128 + 128)
                                for esc in range(8):
                                    pr.op("tensor", lambda e, pw=pw, colr=colr,
                                          oc=oc, esc=esc,
                                          st=(oc % 4 == 0 and esc == 0):
                                          e.matmul(pw[:, colr],
                                                   EOUT[:, esc * 1024 + oc * 128: esc * 1024 + (oc + 1) * 128],
                                                   XT[:, 1024 + esc * 128: 1024 + (esc + 1) * 128],
                                                   start=st, stop=(esc == 7)),
                                          waits=([fv7] if (esc == 0 and oc == 0) else []))
                            fmm7 = pr.fence("tensor")
                            for oc in range(8):
                                pw = PF4 if oc < 4 else PF5
                                colr = slice((oc % 4) * 128, (oc % 4) * 128 + 128)
                                pr.op("vector", lambda e, pw=pw, colr=colr, oc=oc,
                                      tc=tc: e.tensor_tensor(
                                          schunk(oc, tc * 128, 128),
                                          schunk(oc, tc * 128, 128),
                                          pw[:, colr], ALU.add),
                                      waits=[fmm7] if oc == 0 else [])

                    _stkE.close()

                fv = pr.fence("vector")
                fs = pr.fence("scalar")
                w_out = pr.dma("sync", "out", lambda e: e.dma_start(
                    out=ap["y"], in_=STREAM[:]), waits=[fv, fs])
                pr.op("sync", None, waits=[w_out])

                pr.lint()
                with nc.Block() as blk2:
                    pr.replay(blk2, SEMS)
        else:
            fv = pr.fence("vector")
            fs = pr.fence("scalar")
            w_out = pr.dma("sync", "out", lambda e: e.dma_start(
                out=ap["y"], in_=STREAM[:]), waits=[fv, fs])
            pr.op("sync", None, waits=[w_out])
            pr.lint()
            with nc.Block() as blk2:
                pr.replay(blk2, SEMS)

    return nc



def _img_fm(x_td):
    tok, feats = x_td.shape
    return np.ascontiguousarray(
        x_td.T.reshape(feats // P, P, tok).transpose(1, 0, 2).reshape(P, -1))


def _wimg(w, kchunks):
    K, N = w.shape
    assert K == kchunks * P
    return np.ascontiguousarray(
        w.reshape(kchunks, P, N).transpose(1, 0, 2).reshape(P, -1))


def _colimg(v):
    n = v.shape[0] // P
    return np.ascontiguousarray(v.reshape(n, P).T)


def prep_blobs(inp):
    f32 = np.float32
    blobs = {}
    segs = []
    for i in range(NSUB):
        wq = np.asarray(inp["L0_attn_Wq"][i], f32).transpose(1, 0, 2).reshape(D, D)
        wk = np.asarray(inp["L0_attn_Wk"][i], f32).transpose(1, 0, 2).reshape(D, D)
        wv = np.asarray(inp["L0_attn_Wv"][i], f32).transpose(1, 0, 2).reshape(D, D)
        wo = np.asarray(inp["L0_attn_Wo"][i], f32)
        w1 = np.asarray(inp["L0_attn_W1"][i], f32)
        w2 = np.asarray(inp["L0_attn_W2"][i], f32)
        segs += [_wimg(wk, DC), _wimg(wv, DC), _wimg(wq, DC), _wimg(wo, DC),
                 _wimg(w1, DC), _wimg(w2, FC)]
    blobs["wblob"] = np.concatenate(segs, axis=1)
    assert blobs["wblob"].shape == (P, WBLOB_COLS)
    bs = []
    for i in range(NSUB):
        img = np.zeros((P, 128), f32)
        img[:, B_BQ:B_BQ + 8] = _colimg(np.asarray(inp["L0_attn_bq"][i], f32).reshape(-1))
        img[:, B_BK:B_BK + 8] = _colimg(np.asarray(inp["L0_attn_bk"][i], f32).reshape(-1))
        img[:, B_BV:B_BV + 8] = _colimg(np.asarray(inp["L0_attn_bv"][i], f32).reshape(-1))
        img[:, B_BO:B_BO + 8] = _colimg(np.asarray(inp["L0_attn_bo"][i], f32))
        img[:, B_B1:B_B1 + 32] = _colimg(np.asarray(inp["L0_attn_b1"][i], f32))
        img[:, B_B2:B_B2 + 8] = _colimg(np.asarray(inp["L0_attn_b2"][i], f32))
        img[:, B_L1S:B_L1S + 8] = _colimg(np.asarray(inp["L0_attn_ln1_s"][i], f32))
        img[:, B_L1B:B_L1B + 8] = _colimg(np.asarray(inp["L0_attn_ln1_b"][i], f32))
        img[:, B_L2S:B_L2S + 8] = _colimg(np.asarray(inp["L0_attn_ln2_s"][i], f32))
        img[:, B_L2B:B_L2B + 8] = _colimg(np.asarray(inp["L0_attn_ln2_b"][i], f32))
        img[:, 104] = EPS_TOP
        img[:, 105] = EPS_SUB
        bs.append(img)
    blobs["bblob"] = np.concatenate(bs, axis=1)
    ib = np.zeros((P, 288), np.float32)
    ib[:, 0:128] = np.eye(P, dtype=np.float32)
    ib[:, IB_ONECOL] = 1
    ib[0, IB_ONEROW:IB_ONEROW + 128] = 1
    blobs["identb"] = ib
    se = np.asarray(inp["L0_ffn_slot_embeds"], f32).reshape(ES, D)
    fsegs = [_wimg(np.ascontiguousarray(se.T), DC)]
    for ex in range(NEXP):
        wg = np.asarray(inp["L0_ffn_Wg"][ex], f32)
        wu = np.asarray(inp["L0_ffn_Wu"][ex], f32)
        wd = np.asarray(inp["L0_ffn_Wd"][ex], f32)
        for h in range(2):
            fsegs.append(_wimg(np.ascontiguousarray(wg[:, h * 1024:(h + 1) * 1024]), DC))
            fsegs.append(_wimg(np.ascontiguousarray(wu[:, h * 1024:(h + 1) * 1024]), DC))
            fsegs.append(_wimg(np.ascontiguousarray(wd[h * 1024:(h + 1) * 1024, :]), DC))
        bimg = np.zeros((P, 64), f32)
        bimg[:, 0:16] = _colimg(np.asarray(inp["L0_ffn_bg"][ex], f32))
        bimg[:, 16:32] = _colimg(np.asarray(inp["L0_ffn_bu"][ex], f32))
        fsegs.append(bimg)
    blobs["fblob"] = np.concatenate(fsegs, axis=1)
    assert blobs["fblob"].shape == (P, FBLOB_COLS)
    fcb = np.zeros((P, 64), f32)
    fcb[:, FC_NG:FC_NG + 8] = _colimg(np.asarray(inp["L0_ffn_norm_g"], f32))
    fcb[:, FC_NB:FC_NB + 8] = _colimg(np.asarray(inp["L0_ffn_norm_b"], f32))
    fcb[:, FC_SG:FC_SG + 8] = _colimg(np.asarray(inp["L0_ffn_slot_g"], f32))
    fcb[:, FC_SB:FC_SB + 8] = _colimg(np.asarray(inp["L0_ffn_slot_b"], f32))
    fcb[0, FC_SCALE] = np.asarray(inp["L0_ffn_scale"], f32).reshape(-1)[0]
    blobs["fcb"] = fcb
    bd = np.asarray(inp["L0_ffn_bd"], f32)
    blobs["bdimg"] = np.ascontiguousarray(
        np.repeat(bd[:, None, :], SLOT, axis=1).transpose(1, 0, 2).reshape(SLOT, NEXP * D))
    return blobs


def kernel(**inputs):
    x = np.asarray(inputs["L0_attn_x"], np.float32)
    blobs = prep_blobs(inputs)
    nc = build_nc(NSUB, True)
    in_maps = []
    for c in range(8):
        m = dict(blobs)
        m["ximg"] = _img_fm(np.ascontiguousarray(x[c]))
        in_maps.append(m)
    res = run_bass_kernel_spmd(nc, in_maps, core_ids=list(range(8)))
    out = np.empty((8, S, D), np.float32)
    for c in range(8):
        yimg = np.asarray(res.results[c]["y"])
        out[c] = yimg.reshape(P, DC, S).transpose(1, 0, 2).reshape(D, S).T
    return out
